# version 08
# baseline (speedup 1.0000x reference)
import math
import numpy as np
import ml_dtypes
from contextlib import ExitStack

import concourse.bass as bass
import concourse.mybir as mybir
from concourse.bass_utils import run_bass_kernel_spmd
from concourse.aot_env import allow_ici_all_to_all

F32 = mybir.dt.float32
BF16 = mybir.dt.bfloat16
AF = mybir.ActivationFunctionType
ALU = mybir.AluOpType
AX = mybir.AxisListType
BF = ml_dtypes.bfloat16

S, E, F, R = 4096, 4096, 11008, 64
TOK = 1024
NHL = 4
QKV = NHL * 128 + 256
FS, FSP, DS = F // 8, 1408, E // 8
SCAL = float(128) ** -0.5
M_H = float(np.float32(1.5 * 2 ** 22))
M_I = float(np.float32(1.5 * 2 ** 23))
EPS = 1e-5
QW = 512

_CACHE = {}


def _alibi_slopes(n=32):
    cp2 = 2 ** int(math.floor(math.log2(n)))
    base = 2.0 ** (-2.0 ** (-(math.log2(cp2) - 3)))
    slopes = base ** np.arange(1, cp2 + 1, dtype=np.float64)
    if cp2 != n:
        eb = 2.0 ** (-2.0 ** (-(math.log2(2 * cp2) - 3)))
        rem = min(cp2, n - cp2)
        slopes = np.concatenate([slopes, eb ** np.arange(1, 1 + 2 * rem, 2, dtype=np.float64)])
    return slopes.astype(np.float32)


class Sem:
    def __init__(self, es, nc, name):
        self.h = es.enter_context(nc.semaphore(name))
        self.n = 0


class Prog:
    def __init__(self):
        self.sched = {k: [] for k in ("sync", "scalar", "vector", "tensor", "gpsimd")}

    def wait(self, eng, sem, val=None):
        v = sem.n if val is None else val
        if v > 0:
            self.sched[eng].append(lambda e, s=sem.h, vv=v: e.wait_ge(s, vv))

    def op(self, eng, fn, sem=None, k=1):
        if sem is None:
            self.sched[eng].append(fn)
        else:
            sem.n += k
            self.sched[eng].append(lambda e, f=fn, s=sem.h, kk=k: f(e).then_inc(s, kk))

    def dma(self, eng, out, in_, sem=None, transpose=False):
        self.op(eng, lambda e, o=out, i=in_, t=transpose: e.dma_start(out=o, in_=i, transpose=t),
                sem=sem, k=16)


def build_program():
    if "nc" in _CACHE:
        return _CACHE["nc"]
    nc = bass.Bass("TRN2", target_bir_lowering=False)

    x_own = nc.dram_tensor("x_own", [TOK, E], F32, kind="ExternalInput").ap()
    wqkv = nc.dram_tensor("wqkv", [E, QKV], F32, kind="ExternalInput").ap()
    wc = nc.dram_tensor("wc", [E, E], F32, kind="ExternalInput").ap()
    wg_sh = nc.dram_tensor("wg_sh", [FS, E], F32, kind="ExternalInput").ap()
    wu_sh = nc.dram_tensor("wu_sh", [FS, E], F32, kind="ExternalInput").ap()
    wd_sh = nc.dram_tensor("wd_sh", [DS, F], F32, kind="ExternalInput").ap()
    scale_sh = nc.dram_tensor("scale_sh", [128, 4], F32, kind="ExternalInput").ap()
    lrA = nc.dram_tensor("lrA", [E, R], F32, kind="ExternalInput").ap()
    lrB = nc.dram_tensor("lrB", [R, E], F32, kind="ExternalInput").ap()
    aux_bias = nc.dram_tensor("aux_bias", [128, NHL * 8 * 32], F32, kind="ExternalInput").ap()
    aux_bq = nc.dram_tensor("aux_bq", [2, NHL * 512], BF16, kind="ExternalInput").ap()
    aux_mask = nc.dram_tensor("aux_mask", [128, 896], F32, kind="ExternalInput").ap()
    aux_ones = nc.dram_tensor("aux_ones", [128, 128], BF16, kind="ExternalInput").ap()
    out_own = nc.dram_tensor("out_own", [TOK, E], F32, kind="ExternalOutput").ap()

    xnown_sc = nc.dram_tensor("xnown_sc", [TOK, E], BF16).ap()
    agout_x = nc.dram_tensor("agout_x", [8 * TOK, E], BF16).ap()
    xq_sc = nc.dram_tensor("xq_sc", [TOK, E], BF16).ap()
    qkT_sc = nc.dram_tensor("qkT_sc", [2 * QKV, S], BF16).ap()
    hq_sc = nc.dram_tensor("hq_sc", [TOK, F], BF16).ap()
    wcout_sc = nc.dram_tensor("wcout_sc", [TOK, E], BF16).ap()
    wgq_sc = nc.dram_tensor("wgq_sc", [FSP, E], BF16).ap()
    wuq_sc = nc.dram_tensor("wuq_sc", [FSP, E], BF16).ap()
    wdq_sc = nc.dram_tensor("wdq_sc", [DS, F], BF16).ap()
    agin_g = nc.dram_tensor("agin_g", [E, FSP], BF16).ap()
    agin_u = nc.dram_tensor("agin_u", [E, FSP], BF16).ap()
    agin_d = nc.dram_tensor("agin_d", [F, DS], BF16).ap()
    agout_g = nc.dram_tensor("agout_g", [8 * E, FSP], BF16).ap()
    agout_u = nc.dram_tensor("agout_u", [8 * E, FSP], BF16).ap()
    agout_d = nc.dram_tensor("agout_d", [8 * F, DS], BF16).ap()
    a2a_in = nc.dram_tensor("a2a_in", [8 * 512, TOK], BF16).ap()
    a2a_out = nc.dram_tensor("a2a_out", [8 * 512, TOK], BF16).ap()
    wc_bf = nc.dram_tensor("wc_bf", [E, E], BF16).ap()

    G8 = [list(range(8))]
    p = Prog()
    SBUF_LIM = 212863

    with ExitStack() as es:
        ARENA = 151552
        arena_base = [None]

        def alloc(name, shape, dtype, at=None, tdest=False):
            if at is None:
                return nc.alloc_sbuf_tensor(name, shape, dtype)
            itemsz = 2 if dtype == BF16 else 4
            per_part = int(np.prod(shape[1:])) * itemsz
            o = arena_base[0] + (at - (1 << 20))
            assert (at - (1 << 20)) + per_part <= ARENA, (name, at - (1 << 20) + per_part)
            return nc.alloc_sbuf_tensor_at(name, shape, dtype, offset=o)

        q_a = alloc("q_a", [128, QW], F32)
        q_c2 = alloc("q_c2", [128, QW], F32)
        q_c4 = alloc("q_c4", [128, QW], F32)
        q_t1 = alloc("q_t1", [128, QW], F32)
        q_in = alloc("q_in", [128, QW], F32)
        q_r1 = alloc("q_r1", [128, QW], BF16)
        q_r2 = alloc("q_r2", [128, QW], BF16)
        q_r3 = alloc("q_r3", [128, QW], BF16)
        q_qa = alloc("q_qa", [128, QW], BF16)
        q_out = alloc("q_out", [128, QW], BF16)
        q_bm = alloc("q_bm", [128, QW // 16], F32)
        q_r6 = alloc("q_r6", [128, QW // 16], F32)
        q_bm6 = alloc("q_bm6", [128, QW // 16], F32)
        bias_sb = alloc("bias_sb", [128, NHL * 8 * 32], F32)
        bq_sb = alloc("bq_sb", [2, NHL * 512], BF16)
        mask_sb = alloc("mask_sb", [128, 896], F32)
        ones_sb = alloc("ones_sb", [128, 128], BF16)
        t1T_sb = alloc("t1T_sb", [R, TOK], BF16)
        stat_a = alloc("stat_a", [128, 8], F32)
        stat_b = alloc("stat_b", [128, 8], F32)
        st_mean = alloc("st_mean", [128, 1], F32)
        st_var = alloc("st_var", [128, 1], F32)
        st_rstd = alloc("st_rstd", [128, 1], F32)
        st_nb = alloc("st_nb", [128, 1], F32)
        ev_bf = [alloc(f"ev_bf{i}", [128, 512], BF16) for i in range(2)]
        scale_sb = alloc("scale_sb", [128, 4], F32)
        rec_sb = alloc("rec_sb", [128, 512], F32)
        lnr_sb = alloc("lnr_sb", [128, 512], F32)
        attn_bf = alloc("attn_bf", [128, 512], BF16)
        lrA_bf = alloc("lrA_bf", [128, 32 * R], BF16)
        lrB_bf = alloc("lrB_bf", [R, E], BF16)
        x_fin = [alloc(f"x_fin{i}", [128, 512], F32) for i in range(2)]
        rs_fin = [alloc(f"rs_fin{i}", [128, 512], BF16) for i in range(2)]
        o_fin = [alloc(f"o_fin{i}", [128, 512], F32) for i in range(2)]
        _arena_t = nc.alloc_sbuf_tensor("overlay_arena", [128, ARENA], mybir.dt.uint8)
        arena_base[0] = nc.lookup_mloc(_arena_t).addr
        OV = 1 << 20

        xb = [alloc(f"xb{i}", [128, E], F32, at=OV + i * 16384) for i in range(2)]
        wqkv_bf = alloc("wqkv_bf", [128, 32 * QKV], BF16, at=OV + 32768)
        rhs_buf = [alloc(f"rhs{i}", [128, 32 * 512], BF16,
                         at=(OV + 0 if i == 0 else OV + 81920)) for i in range(2)]
        kT2 = alloc("kT2", [128, 2 * S], BF16, at=OV)
        v2 = alloc("v2", [128, 2 * S], BF16, at=OV + 16384)
        qT_buf = [alloc(f"qT{i}", [128, 512], BF16, at=OV + 32768 + i * 1024) for i in range(2)]
        probs_sb = alloc("probs_sb", [128, 8 * 512], BF16, at=OV + 34816)
        wq_in = [alloc(f"wq_in{i}", [128, QW], F32, at=OV + 131072 + i * 2048) for i in range(2)]
        wq_tr = [alloc(f"wq_tr{i}", [128, FSP], BF16, at=OV + 135168 + i * 2816) for i in range(2)]
        a2a_res = alloc("a2a_res", [128, 8 * 32 * 128], BF16, at=OV)
        wc_rhs = [alloc(f"wc_rhs{i}", [128, 32 * 512], BF16, at=OV + 65536 + i * 32768) for i in range(2)]
        xqT_sb = alloc("xqT_sb", [128, 32 * TOK], BF16, at=OV)
        wgu_buf = [alloc(f"wgu{i}", [128, 32 * 512], BF16, at=OV + 65536 + i * 32768) for i in range(2)]
        gs_sb = alloc("gs_sb", [128, 8 * 512], F32, at=OV + 131072)
        hqT_sb = alloc("hqT_sb", [128, 86 * 512], BF16, at=OV)
        wd_buf = [alloc(f"wd_buf{i}", [128, 16 * 512], BF16, at=OV + 90112 + i * 16384) for i in range(2)]

        psA = es.enter_context(nc.psum_tensor("psA", [128, 2048], F32))
        psB = es.enter_context(nc.psum_tensor("psB", [128, 2048], F32))

        sn = ("aux ldx stat app xnw xqv xqw agx wqkvl rhs mm ev qkw lora lrb scl "
              "kv qt st exp msk pv rec evac a2w a2a "
              "wqi wqv wqw wqt agw ag0 ag1 ag2 "
              "cw ca cmm cev cwr "
              "xqt gw uw gmm umm sil hmul hq hqw "
              "hqt dw dgrp dmm fin fld out vsq ssq cwp").split()
        s = {n: Sem(es, nc, "s_" + n) for n in sn}

        def emit_quant(src_ap, w, dst_ap, pp_scale=None, sem_out=None):
            nb = w // 16
            src3 = src_ap.rearrange("p (b k) -> p b k", k=16)
            p.op("vector", lambda v, ss=src3, n=nb: v.tensor_reduce(
                q_bm.ap()[:, :n], ss, AX.X, ALU.max, apply_absolute_value=True),
                sem=s["vsq"])
            p.wait("vector", s["vsq"])
            p.op("vector", lambda v, n=nb: v.tensor_scalar(
                q_bm.ap()[:, :n], q_bm.ap()[:, :n], 1e-30, None, ALU.max),
                sem=s["vsq"])
            p.wait("vector", s["vsq"])
            p.op("vector", lambda v, n=nb: v.reciprocal(q_r6.ap()[:, :n], q_bm.ap()[:, :n]),
                 sem=s["vsq"])
            if pp_scale is None:
                p.op("vector", lambda v, n=nb: v.tensor_scalar(
                    q_bm6.ap()[:, :n], q_bm.ap()[:, :n], 1.0 / 6.0, None, ALU.mult))
            else:
                p.op("vector", lambda v, n=nb, pa=pp_scale: v.tensor_scalar(
                    q_bm6.ap()[:, :n], q_bm.ap()[:, :n], 1.0 / 6.0, pa, ALU.mult, ALU.mult))
            p.wait("vector", s["vsq"])
            r6b = q_r6.ap()[:, :nb].unsqueeze(2).broadcast_to([128, nb, 16])
            bm6b = q_bm6.ap()[:, :nb].unsqueeze(2).broadcast_to([128, nb, 16])
            a3 = q_a.ap()[:, :w].rearrange("p (b k) -> p b k", k=16)
            p.op("vector", lambda v, ss=src3, r=r6b, a=a3: v.scalar_tensor_tensor(
                a, ss, 6.0, r, ALU.mult, ALU.mult))
            p.op("vector", lambda v, w=w: v.tensor_scalar(
                q_c2.ap()[:, :w], q_a.ap()[:, :w], -2.0, 2.0, ALU.max, ALU.min))
            p.op("vector", lambda v, w=w: v.tensor_scalar(
                q_c4.ap()[:, :w], q_a.ap()[:, :w], -4.0, 4.0, ALU.max, ALU.min))
            p.op("vector", lambda v, w=w: v.tensor_scalar(
                q_r1.ap()[:, :w], q_c2.ap()[:, :w], M_H, -M_H, ALU.add, ALU.add))
            p.op("vector", lambda v, w=w: v.tensor_sub(
                q_t1.ap()[:, :w], q_c4.ap()[:, :w], q_c2.ap()[:, :w]))
            p.op("vector", lambda v, w=w: v.tensor_scalar(
                q_r2.ap()[:, :w], q_t1.ap()[:, :w], M_I, -M_I, ALU.add, ALU.add))
            p.op("vector", lambda v, w=w: v.tensor_sub(
                q_c2.ap()[:, :w], q_a.ap()[:, :w], q_c4.ap()[:, :w]))
            p.op("vector", lambda v, w=w: v.tensor_scalar(
                q_t1.ap()[:, :w], q_c2.ap()[:, :w], 0.5, M_I, ALU.mult, ALU.add))
            p.op("vector", lambda v, w=w: v.tensor_scalar(
                q_r3.ap()[:, :w], q_t1.ap()[:, :w], -M_I, 2.0, ALU.add, ALU.mult))
            p.op("vector", lambda v, w=w: v.tensor_add(
                q_qa.ap()[:, :w], q_r1.ap()[:, :w], q_r2.ap()[:, :w]))
            p.op("vector", lambda v, w=w: v.tensor_add(
                q_r1.ap()[:, :w], q_qa.ap()[:, :w], q_r3.ap()[:, :w]))
            d3 = dst_ap.rearrange("p (b k) -> p b k", k=16)
            r13 = q_r1.ap()[:, :w].rearrange("p (b k) -> p b k", k=16)
            p.op("vector", lambda v, d=d3, r=r13, m=bm6b: v.tensor_tensor(d, r, m, ALU.mult),
                 sem=sem_out)

        p.dma("sync", bias_sb.ap(), aux_bias, sem=s["aux"])
        p.dma("sync", bq_sb.ap(), aux_bq, sem=s["aux"])
        p.dma("sync", mask_sb.ap(), aux_mask, sem=s["aux"])
        p.dma("sync", ones_sb.ap(), aux_ones, sem=s["aux"])
        p.dma("sync", scale_sb.ap(), scale_sh, sem=s["scl"])
        p.dma("gpsimd", wqkv_bf.ap().rearrange("p (kk n) -> p kk n", kk=32),
              wqkv.rearrange("(kk p) n -> p kk n", p=128), sem=s["wqkvl"])
        p.dma("gpsimd", lrA_bf.ap().rearrange("p (kk r) -> p kk r", kk=32),
              lrA.rearrange("(kk p) r -> p kk r", p=128), sem=s["lrb"])
        p.dma("gpsimd", lrB_bf.ap(), lrB, sem=s["lrb"])

        NCK = E // QW
        for i in range(8):
            par = i % 2
            if i >= 2:
                p.wait("sync", s["app"], 2 * NCK * (i - 1))
            p.dma("sync", xb[par].ap(), x_own[i * 128:(i + 1) * 128, :], sem=s["ldx"])
            p.wait("vector", s["ldx"], (i + 1) * 16)
            for k in range(NCK):
                ck = xb[par].ap()[:, k * QW:(k + 1) * QW]
                last = (k == NCK - 1)
                p.op("vector", lambda v, c=ck, kk=k: v.scalar_tensor_tensor(
                    q_t1.ap()[:, :QW], c, 1.0, c, ALU.bypass, ALU.mult,
                    accum_out=stat_b.ap()[:, kk:kk + 1]),
                    sem=(s["vsq"] if last else None))
                p.op("vector", lambda v, c=ck, kk=k: v.tensor_reduce(
                    stat_a.ap()[:, kk:kk + 1], c, AX.X, ALU.add),
                    sem=(s["vsq"] if last else None))
            p.wait("vector", s["vsq"])
            p.op("vector", lambda v: v.tensor_reduce(st_mean.ap(), stat_a.ap()[:, 0:NCK], AX.X, ALU.add), sem=s["vsq"])
            p.op("vector", lambda v: v.tensor_reduce(st_var.ap(), stat_b.ap()[:, 0:NCK], AX.X, ALU.add), sem=s["vsq"])
            p.wait("vector", s["vsq"])
            p.op("vector", lambda v: v.tensor_scalar(st_mean.ap(), st_mean.ap(), 1.0 / E, None, ALU.mult), sem=s["vsq"])
            p.op("vector", lambda v: v.tensor_scalar(st_var.ap(), st_var.ap(), 1.0 / E, None, ALU.mult), sem=s["vsq"])
            p.wait("vector", s["vsq"])
            p.op("vector", lambda v: v.tensor_scalar(
                q_t1.ap()[:, 0:1], st_mean.ap(), st_mean.ap()[:, 0:1], None, ALU.mult), sem=s["vsq"])
            p.wait("vector", s["vsq"])
            p.op("vector", lambda v: v.tensor_sub(
                st_var.ap(), st_var.ap(), q_t1.ap()[:, 0:1]), sem=s["vsq"])
            p.wait("vector", s["vsq"])
            p.op("vector", lambda v: v.tensor_scalar(
                st_var.ap(), st_var.ap(), EPS, None, ALU.add), sem=s["stat"])
            p.wait("scalar", s["stat"])
            p.op("scalar", lambda sc: sc.activation(st_rstd.ap(), st_var.ap(), AF.Ln), sem=s["ssq"])
            p.wait("scalar", s["ssq"])
            p.op("scalar", lambda sc: sc.activation(st_rstd.ap(), st_rstd.ap(), AF.Exp, scale=-0.5),
                 sem=s["stat"])
            p.wait("vector", s["stat"])
            p.op("vector", lambda v: v.tensor_scalar(
                st_nb.ap(), st_mean.ap(), st_rstd.ap()[:, 0:1], -1.0, ALU.mult, ALU.mult),
                sem=s["stat"])
            p.wait("vector", s["stat"])
            p.wait("scalar", s["stat"])
            nbb = st_nb.ap()[:, 0:1].broadcast_to([128, QW])
            for k in range(NCK):
                ck = xb[par].ap()[:, k * QW:(k + 1) * QW]
                p.wait("vector", s["xqv"])
                p.op("vector", lambda v, c=ck, nb=nbb: v.scalar_tensor_tensor(
                    q_in.ap()[:, :QW], c, st_rstd.ap()[:, 0:1], nb, ALU.mult, ALU.add),
                    sem=s["app"])
                p.wait("vector", s["xnw"])
                p.op("vector", lambda v, c=ck, nb=nbb: v.scalar_tensor_tensor(
                    q_qa.ap()[:, :QW], c, st_rstd.ap()[:, 0:1], nb, ALU.mult, ALU.add),
                    sem=s["app"])
                p.wait("scalar", s["app"])
                p.dma("scalar", xnown_sc[i * 128:(i + 1) * 128, k * QW:(k + 1) * QW],
                      q_qa.ap()[:, :QW], sem=s["xnw"])
                p.wait("vector", s["xqw"])
                p.wait("vector", s["xnw"])
                emit_quant(q_in.ap()[:, :QW], QW, q_out.ap()[:, :QW], sem_out=s["xqv"])
                if k == NCK - 1:
                    p.op("vector", lambda v: v.tensor_copy(
                        q_out.ap()[:, QW - 128:QW], q_in.ap()[:, QW - 128:QW]), sem=s["xqv"])
                p.wait("sync", s["xqv"])
                p.dma("sync", xq_sc[i * 128:(i + 1) * 128, k * QW:(k + 1) * QW],
                      q_out.ap()[:, :QW], sem=s["xqw"])
        p.wait("gpsimd", s["xnw"])
        p.op("gpsimd", lambda g: g.collective_compute(
            "AllGather", ALU.bypass, replica_groups=G8, ins=[xnown_sc], outs=[agout_x]),
            sem=s["agx"])

        wq_units = []

        def mk_unit(srcd, dstd, r0, rh, srh, colk, w, pp, pt):
            def load(par):
                p.dma("scalar", wq_in[par].ap()[0:rh, :w], srcd[r0:r0 + rh, colk:colk + w],
                      sem=s["wqi"])

            def quant(par):
                p.wait("vector", s["wqi"])
                p.wait("vector", s["xqw"])
                p.wait("vector", s["wqw"])
                if pp is not None:
                    p.wait("vector", s["scl"], 16)
                emit_quant(wq_in[par].ap()[0:128, :w], w, q_out.ap()[0:128, :w],
                           pp_scale=pp, sem_out=s["wqv"])
                if pt:
                    if pp is None:
                        p.op("vector", lambda v, ww=w: v.tensor_copy(
                            q_out.ap()[:, ww - 128:ww], wq_in[par].ap()[:, ww - 128:ww]),
                            sem=s["wqv"])
                    else:
                        p.op("vector", lambda v, ww=w, pa=pp: v.tensor_scalar(
                            q_out.ap()[:, ww - 128:ww], wq_in[par].ap()[:, ww - 128:ww],
                            pa, None, ALU.mult), sem=s["wqv"])
                p.wait("scalar", s["wqv"])
                p.dma("scalar", dstd[r0:r0 + srh, colk:colk + w], q_out.ap()[0:srh, :w],
                      sem=s["wqw"])
            return load, quant

        for (srcd, dstd, nr, pp_on, fax) in ((wg_sh, wgq_sc, FS, False, False),
                                             (wu_sh, wuq_sc, FS, False, False),
                                             (wd_sh, wdq_sc, DS, True, True)):
            ncols = F if fax else E
            r0 = 0
            while r0 < nr:
                rh = min(128, nr - r0)
                srh = 128 if (nr == FS and r0 + 128 == FSP) else rh
                pp = scale_sb.ap()[:, r0 // 128:r0 // 128 + 1] if pp_on else None
                for colk in range(0, ncols, QW):
                    w = min(QW, ncols - colk)
                    pt = (colk + w == ncols)
                    wq_units.append(mk_unit(srcd, dstd, r0, rh, srh, colk, w, pp, pt))
                r0 += 128
        wq_state = {"i": 0, "loaded": 0}
        wqv_after = []

        def wq_load_next():
            j = wq_state["loaded"]
            if j < len(wq_units):
                if j >= 2:
                    p.wait("scalar", s["wqv"], wqv_after[j - 2])
                wq_units[j][0](j % 2)
                wq_state["loaded"] += 1

        def drain_wq(n):
            for _ in range(n):
                i = wq_state["i"]
                if i >= len(wq_units):
                    return
                if wq_state["loaded"] == i:
                    wq_load_next()
                wq_load_next()
                wq_units[i][1](i % 2)
                wqv_after.append(s["wqv"].n)
                wq_state["i"] += 1

        p.wait("tensor", s["wqkvl"])
        p.wait("sync", s["agx"], 1)
        p.wait("sync", s["app"])
        nmm = 0
        mm_after_chunk = []
        for ch in range(16):
            bb, tc = ch // 8, ch % 8
            par = ch % 2
            if ch >= 2:
                p.wait("sync", s["mm"], mm_after_chunk[ch - 2])
            for kk in range(32):
                p.dma("sync", rhs_buf[par].ap()[:, kk * 512:(kk + 1) * 512],
                      agout_x[bb * S + tc * 512:bb * S + (tc + 1) * 512, kk * 128:(kk + 1) * 128],
                      sem=s["rhs"], transpose=True)
            p.wait("tensor", s["rhs"])
            for m in range(6):
                slot = nmm % 4
                ps = psA.ap()[:, slot * 512:(slot + 1) * 512]
                if nmm >= 4:
                    p.wait("tensor", s["ev"], nmm - 3)
                for kk in range(32):
                    p.op("tensor", lambda t, o=ps,
                         l=wqkv_bf.ap()[:, kk * QKV + m * 128:kk * QKV + (m + 1) * 128],
                         r=rhs_buf[par].ap()[:, kk * 512:(kk + 1) * 512],
                         st=(kk == 0), sp=(kk == 31): t.matmul(o, l, r, start=st, stop=sp),
                         sem=(s["mm"] if kk == 31 else None))
                p.wait("vector", s["mm"])
                if nmm >= 2:
                    p.wait("vector", s["qkw"], (nmm - 1) * 16)
                evp = ev_bf[nmm % 2].ap()
                p.op("vector", lambda v, o=evp, i=ps: v.tensor_copy(o, i), sem=s["ev"])
                p.wait("scalar", s["ev"])
                p.dma("scalar", qkT_sc[bb * QKV + m * 128:bb * QKV + (m + 1) * 128,
                                       tc * 512:(tc + 1) * 512], evp, sem=s["qkw"])
                nmm += 1
            mm_after_chunk.append(s["mm"].n)
            drain_wq(3)
        p.wait("tensor", s["lrb"], 32)
        for tc in range(2):
            par = tc % 2
            p.wait("sync", s["mm"], mm_after_chunk[14 + tc])
            for kk in range(32):
                p.dma("sync", rhs_buf[par].ap()[:, kk * 512:(kk + 1) * 512],
                      xnown_sc[tc * 512:(tc + 1) * 512, kk * 128:(kk + 1) * 128],
                      sem=s["rhs"], transpose=True)
            p.wait("tensor", s["rhs"])
            ps = psB.ap()[0:R, tc * 512:(tc + 1) * 512]
            for kk in range(32):
                p.op("tensor", lambda t, o=ps, l=lrA_bf.ap()[:, kk * R:(kk + 1) * R],
                     r=rhs_buf[par].ap()[:, kk * 512:(kk + 1) * 512],
                     st=(kk == 0), sp=(kk == 31): t.matmul(o, l, r, start=st, stop=sp),
                     sem=(s["mm"] if kk == 31 else None))
            p.wait("vector", s["mm"])
            p.op("vector", lambda v, o=t1T_sb.ap()[:, tc * 512:(tc + 1) * 512], i=ps:
                 v.tensor_copy(o, i), sem=s["lora"])

        p.wait("sync", s["qkw"])
        p.wait("sync", s["lora"])
        for bb in range(2):
            p.dma("sync", kT2.ap()[:, bb * S:(bb + 1) * S],
                  qkT_sc[bb * QKV + 512:bb * QKV + 640, :], sem=s["kv"])
            for t in range(32):
                p.dma("sync", v2.ap()[:, bb * S + t * 128:bb * S + (t + 1) * 128],
                      qkT_sc[bb * QKV + 640:bb * QKV + 768, t * 128:(t + 1) * 128],
                      sem=s["kv"], transpose=True)
        p.wait("tensor", s["kv"])
        p.wait("tensor", s["ev"])
        p.wait("tensor", s["lora"])
        for eng in ("tensor", "scalar", "vector"):
            p.wait(eng, s["aux"])
        for i in range(8):
            p.dma("gpsimd", wc_bf[i * 512:(i + 1) * 512, :],
                  wc[i * 512:(i + 1) * 512, :], sem=s["cwp"])

        it = 0
        st_after_it = []
        for bb in range(2):
            for h in range(NHL):
                for c in range(8):
                    nk = 4 * (c + 1)
                    par = it % 2
                    apar = it % 2
                    if it >= 2:
                        p.wait("sync", s["st"], st_after_it[it - 2])
                    p.dma("sync", qT_buf[par].ap(),
                          qkT_sc[bb * QKV + h * 128:bb * QKV + (h + 1) * 128,
                                 c * 512:(c + 1) * 512], sem=s["qt"])
                    p.wait("tensor", s["qt"])
                    attn_ps = psB.ap()[:, apar * 512:(apar + 1) * 512]
                    rows_ps = psB.ap()[:, (2 + apar) * 512:(3 + apar) * 512]
                    if it >= 2:
                        p.wait("tensor", s["evac"], it - 1)
                    bias_base = (h * 8 + c) * 32
                    ngrp = (nk + 3) // 4
                    for grp in range(ngrp):
                        tlist = list(range(grp * 4, min(grp * 4 + 4, nk)))
                        if grp >= 1:
                            p.wait("tensor", s["exp"])
                        if grp >= 2 or (grp == 0 and it >= 1):
                            p.wait("scalar", s["pv"])
                        for t in tlist:
                            slot = t % 4
                            stp = psA.ap()[:, slot * 512:(slot + 1) * 512]
                            p.op("tensor", lambda tt, o=stp,
                                 l=kT2.ap()[:, bb * S + t * 128:bb * S + (t + 1) * 128],
                                 r=qT_buf[par].ap(): tt.matmul(o, l, r, start=True, stop=False))
                            p.op("tensor", lambda tt, o=stp, l=ones_sb.ap()[0:2, :],
                                 r=bq_sb.ap()[0:2, h * 512:(h + 1) * 512]:
                                 tt.matmul(o, l, r, start=False, stop=True), sem=s["st"])
                        msk_base = s["msk"].n
                        for t in tlist:
                            if t < nk - 4:
                                continue
                            slot = t % 4
                            stp = psA.ap()[:, slot * 512:(slot + 1) * 512]
                            d = t - (nk - 4)
                            p.wait("vector", s["st"], s["st"].n - (tlist[-1] - t))
                            p.op("vector", lambda v, o=stp,
                                 m=mask_sb.ap()[:, 384 - 128 * d:896 - 128 * d]:
                                 v.tensor_tensor(o, o, m, ALU.min), sem=s["msk"])
                        for t in tlist:
                            slot = t % 4
                            stp = psA.ap()[:, slot * 512:(slot + 1) * 512]
                            prb = probs_sb.ap()[:, (t % 8) * 512:(t % 8) * 512 + 512]
                            p.wait("scalar", s["st"], s["st"].n - (tlist[-1] - t))
                            if t >= nk - 4:
                                p.wait("scalar", s["msk"], msk_base + (t - (nk - 4) + 1))
                            p.op("scalar", lambda sc, o=prb, i=stp,
                                 ba=bias_sb.ap()[:, bias_base + t:bias_base + t + 1]:
                                 sc.activation(o, i, AF.Exp, bias=ba, scale=SCAL), sem=s["exp"])
                        p.wait("tensor", s["exp"])
                        for t in tlist:
                            prb = probs_sb.ap()[:, (t % 8) * 512:(t % 8) * 512 + 512]
                            p.op("tensor", lambda tt, o=attn_ps,
                                 l=v2.ap()[:, bb * S + t * 128:bb * S + (t + 1) * 128],
                                 r=prb, st=(t == 0), sp=(t == nk - 1):
                                 tt.matmul(o, l, r, start=st, stop=sp), sem=s["pv"])
                            p.op("tensor", lambda tt, o=rows_ps, l=ones_sb.ap(), r=prb,
                                 st=(t == 0), sp=(t == nk - 1):
                                 tt.matmul(o, l, r, start=st, stop=sp), sem=s["pv"])
                    st_after_it.append(s["st"].n)
                    p.wait("scalar", s["pv"])
                    if it >= 1:
                        p.wait("scalar", s["evac"], it)
                    p.op("scalar", lambda sc, i=rows_ps: sc.activation(lnr_sb.ap(), i, AF.Ln))
                    p.op("scalar", lambda sc: sc.activation(rec_sb.ap(), lnr_sb.ap(), AF.Exp,
                                                            scale=-1.0), sem=s["rec"])
                    p.wait("vector", s["rec"])
                    if it >= 1:
                        p.wait("vector", s["a2w"], it * 16)
                    p.op("vector", lambda v, i=attn_ps: v.tensor_tensor(
                        attn_bf.ap(), i, rec_sb.ap(), ALU.mult), sem=s["evac"])
                    p.wait("scalar", s["evac"])
                    j = bb * 4 + c // 2
                    p.dma("scalar", a2a_in[j * 512 + h * 128:j * 512 + (h + 1) * 128,
                                           (c % 2) * 512:(c % 2) * 512 + 512],
                          attn_bf.ap(), sem=s["a2w"])
                    it += 1
                    drain_wq(2)

        p.wait("gpsimd", s["a2w"])

        def _a2a_op(g):
            with allow_ici_all_to_all():
                return g.collective_compute(
                    "AllToAll", ALU.bypass, replica_groups=G8, ins=[a2a_in], outs=[a2a_out])
        p.op("gpsimd", _a2a_op, sem=s["a2a"])

        p.wait("sync", s["wqw"], 176 * 16)
        jj = 0
        for (src_, agi, njt, wrow) in ((wgq_sc, agin_g, 32, FSP), (wuq_sc, agin_u, 32, FSP)):
            for j in range(njt):
                par = jj % 2
                if jj >= 2:
                    p.wait("sync", s["agw"], (jj - 1) * 16)
                p.dma("sync", wq_tr[par].ap()[:, :wrow], src_[:, j * 128:(j + 1) * 128],
                      sem=s["wqt"], transpose=True)
                p.wait("scalar", s["wqt"])
                p.dma("scalar", agi[j * 128:(j + 1) * 128, :], wq_tr[par].ap()[:, :wrow],
                      sem=s["agw"])
                jj += 1
        p.wait("gpsimd", s["agw"], 64 * 16)
        p.op("gpsimd", lambda g: g.collective_compute(
            "AllGather", ALU.bypass, replica_groups=G8,
            ins=[agin_g], outs=[agout_g]), sem=s["ag0"])
        p.op("gpsimd", lambda g: g.collective_compute(
            "AllGather", ALU.bypass, replica_groups=G8,
            ins=[agin_u], outs=[agout_u]), sem=s["ag1"])

        p.wait("sync", s["a2a"], 1)
        for tt in range(8):
            for kk in range(32):
                p.dma("sync", a2a_res.ap()[:, (tt * 32 + kk) * 128:(tt * 32 + kk + 1) * 128],
                      a2a_out[kk * 128:(kk + 1) * 128, tt * 128:(tt + 1) * 128],
                      sem=s["ca"])
        p.wait("sync", s["cwp"])
        for dc0 in range(2):
            for kk in range(32):
                p.dma("sync", wc_rhs[dc0].ap()[:, kk * 512:(kk + 1) * 512],
                      wc_bf[kk * 128:(kk + 1) * 128, dc0 * 512:(dc0 + 1) * 512],
                      sem=s["cw"])
        p.wait("tensor", s["ca"])
        ncm = 0
        for dc in range(8):
            par = dc % 2
            p.wait("tensor", s["cw"], (dc + 1) * 32 * 16)
            for tt in range(8):
                slot = ncm % 4
                ps = psB.ap()[:, slot * 512:(slot + 1) * 512]
                if ncm >= 4:
                    p.wait("tensor", s["cev"], ncm - 3)
                for kk in range(32):
                    p.op("tensor", lambda t, o=ps,
                         l=a2a_res.ap()[:, (tt * 32 + kk) * 128:(tt * 32 + kk + 1) * 128],
                         r=wc_rhs[par].ap()[:, kk * 512:(kk + 1) * 512],
                         st=(kk == 0), sp=(kk == 31): t.matmul(o, l, r, start=st, stop=sp),
                         sem=(s["cmm"] if kk == 31 else None))
                p.wait("scalar", s["cmm"])
                if ncm >= 2:
                    p.wait("scalar", s["cwr"], (ncm - 1) * 16)
                evp = ev_bf[ncm % 2].ap()
                p.op("scalar", lambda sc, o=evp, i=ps: sc.activation(o, i, AF.Copy), sem=s["cev"])
                p.wait("scalar", s["cev"])
                p.dma("scalar", wcout_sc[tt * 128:(tt + 1) * 128, dc * 512:(dc + 1) * 512],
                      evp, sem=s["cwr"])
                p.wait("sync", s["cwr"])
                p.wait("tensor", s["cwr"])
                ncm += 1
            if dc + 2 < 8:
                p.wait("sync", s["cmm"], (dc + 1) * 8)
                for kk in range(32):
                    p.dma("sync", wc_rhs[par].ap()[:, kk * 512:(kk + 1) * 512],
                          wc_bf[kk * 128:(kk + 1) * 128, (dc + 2) * 512:(dc + 3) * 512],
                          sem=s["cw"])
            drain_wq(11)
        drain_wq(10 ** 9)

        p.wait("sync", s["wqw"])
        jj2 = 0
        for j in range(86):
            par = jj2 % 2
            if jj2 >= 2:
                p.wait("sync", s["agw"], (64 + jj2 - 1) * 16)
            p.dma("sync", wq_tr[par].ap()[:, :DS], wdq_sc[:, j * 128:(j + 1) * 128],
                  sem=s["wqt"], transpose=True)
            p.wait("scalar", s["wqt"])
            p.dma("scalar", agin_d[j * 128:(j + 1) * 128, :], wq_tr[par].ap()[:, :DS],
                  sem=s["agw"])
            jj2 += 1
        p.wait("gpsimd", s["agw"])
        p.op("gpsimd", lambda g: g.collective_compute(
            "AllGather", ALU.bypass, replica_groups=G8, ins=[agin_d], outs=[agout_d]),
            sem=s["ag2"])

        p.wait("sync", s["cmm"])
        p.wait("sync", s["agw"])
        for j in range(32):
            p.dma("sync", xqT_sb.ap()[:, j * TOK:(j + 1) * TOK], xq_sc[:, j * 128:(j + 1) * 128],
                  sem=s["xqt"], transpose=True)
        p.wait("tensor", s["xqt"])
        p.wait("tensor", s["ag0"], 1)
        p.wait("tensor", s["ag1"], 1)
        p.wait("sync", s["ag0"], 1)
        p.wait("sync", s["ag1"], 1)
        p.wait("gpsimd", s["cmm"])
        p.wait("scalar", s["cmm"])
        p.wait("scalar", s["agw"])
        p.wait("vector", s["wqw"])
        chunks = [(r, o, w) for r in range(8) for (o, w) in ((0, 512), (512, 512), (1024, 352))]
        ng = 0
        for (r, coff, cw) in chunks:
            if ng >= 1:
                p.wait("sync", s["gmm"], 8 * ng)
                p.wait("sync", s["umm"], 8 * ng)
            for kk in range(32):
                p.dma("sync", wgu_buf[0].ap()[:, kk * cw:(kk + 1) * cw],
                      agout_g[r * E + kk * 128:r * E + (kk + 1) * 128, coff:coff + cw],
                      sem=s["gw"])
                p.dma("sync", wgu_buf[1].ap()[:, kk * cw:(kk + 1) * cw],
                      agout_u[r * E + kk * 128:r * E + (kk + 1) * 128, coff:coff + cw],
                      sem=s["uw"])
            p.wait("tensor", s["gw"])
            p.wait("scalar", s["hmul"])
            for tt in range(8):
                slot = tt % 2
                ps = psA.ap()[:, slot * 512:slot * 512 + cw]
                if tt >= 2:
                    p.wait("tensor", s["sil"], s["sil"].n - 1)
                for kk in range(32):
                    p.op("tensor", lambda t, o=ps,
                         l=xqT_sb.ap()[:, kk * TOK + tt * 128:kk * TOK + (tt + 1) * 128],
                         r=wgu_buf[0].ap()[:, kk * cw:(kk + 1) * cw],
                         st=(kk == 0), sp=(kk == 31): t.matmul(o, l, r, start=st, stop=sp),
                         sem=(s["gmm"] if kk == 31 else None))
                p.wait("scalar", s["gmm"])
                p.op("scalar", lambda sc, o=gs_sb.ap()[:, tt * 512:tt * 512 + cw], i=ps:
                     sc.activation(o, i, AF.Silu), sem=s["sil"])
            p.wait("tensor", s["uw"])
            for tt in range(8):
                slot = 2 + tt % 2
                ps = psA.ap()[:, slot * 512:slot * 512 + cw]
                if tt >= 2:
                    p.wait("tensor", s["hmul"], s["hmul"].n - 1)
                for kk in range(32):
                    p.op("tensor", lambda t, o=ps,
                         l=xqT_sb.ap()[:, kk * TOK + tt * 128:kk * TOK + (tt + 1) * 128],
                         r=wgu_buf[1].ap()[:, kk * cw:(kk + 1) * cw],
                         st=(kk == 0), sp=(kk == 31): t.matmul(o, l, r, start=st, stop=sp),
                         sem=(s["umm"] if kk == 31 else None))
                p.wait("vector", s["umm"])
                p.wait("vector", s["sil"])
                p.wait("vector", s["hq"])
                p.op("vector", lambda v, i=ps, g=gs_sb.ap()[:, tt * 512:tt * 512 + cw],
                     o=q_in.ap()[:, :cw]: v.tensor_tensor(o, g, i, ALU.mult), sem=s["hmul"])
                p.wait("vector", s["hqw"])
                emit_quant(q_in.ap()[:, :cw], cw, q_out.ap()[:, :cw], sem_out=s["hq"])
                if r == 7 and coff == 1024:
                    p.op("vector", lambda v, ww=cw: v.tensor_copy(
                        q_out.ap()[:, ww - 128:ww], q_in.ap()[:, ww - 128:ww]), sem=s["hq"])
                p.wait("sync", s["hq"])
                p.dma("sync", hq_sc[tt * 128:(tt + 1) * 128, r * FS + coff:r * FS + coff + cw],
                      q_out.ap()[:, :cw], sem=s["hqw"])
            ng += 1

        p.wait("sync", s["hqw"])
        p.wait("sync", s["cwr"])
        p.wait("tensor", s["ag2"], 1)
        p.wait("sync", s["gmm"])
        p.wait("sync", s["umm"])
        p.wait("gpsimd", s["umm"])
        p.wait("gpsimd", s["ag2"], 1)
        WDG = [16, 16, 16, 16, 16, 6]
        nfm = 0
        gcnt = 0
        dgrp_after = []
        for tg in range(2):
            if tg == 1:
                p.wait("sync", s["dgrp"], dgrp_after[-1])
            for ft in range(86):
                p.dma("sync", hqT_sb.ap()[:, ft * 512:(ft + 1) * 512],
                      hq_sc[tg * 512:(tg + 1) * 512, ft * 128:(ft + 1) * 128],
                      sem=s["hqt"], transpose=True)
            p.wait("tensor", s["hqt"])
            for dc in range(8):
                gbase = 0
                for gi, gn in enumerate(WDG):
                    par = gcnt % 2
                    if gcnt >= 2:
                        p.wait("gpsimd", s["dgrp"], gcnt - 1)
                    p.dma("gpsimd", wd_buf[par].ap()[:, :gn * 512]
                          .rearrange("p (kk d) -> p kk d", kk=gn),
                          agout_d[dc * F + gbase * 128:dc * F + (gbase + gn) * 128, :]
                          .rearrange("(kk p) d -> p kk d", p=128), sem=s["dw"])
                    p.wait("tensor", s["dw"])
                    for tt in range(4):
                        slot = tt
                        ps = psB.ap()[:, slot * 512:(slot + 1) * 512]
                        if gi == 0 and nfm + tt >= 4:
                            p.wait("tensor", s["fin"], (nfm + tt) - 3)
                        for k2 in range(gn):
                            kk = gbase + k2
                            last = (tt == 3 and k2 == gn - 1)
                            p.op("tensor", lambda t, o=ps,
                                 l=hqT_sb.ap()[:, kk * 512 + tt * 128:kk * 512 + (tt + 1) * 128],
                                 r=wd_buf[par].ap()[:, k2 * 512:(k2 + 1) * 512],
                                 st=(kk == 0), sp=False: t.matmul(o, l, r, start=st, stop=sp),
                                 sem=(s["dgrp"] if last else None))
                    gbase += gn
                    gcnt += 1
                    dgrp_after.append(s["dgrp"].n)
                for tt in range(4):
                    ps = psB.ap()[:, tt * 512:(tt + 1) * 512]
                    row0 = tg * 512 + tt * 128
                    p.op("tensor", lambda t, o=ps,
                         l=t1T_sb.ap()[:, row0:row0 + 128],
                         r=lrB_bf.ap()[:, dc * 512:(dc + 1) * 512]:
                         t.matmul(o, l, r, start=False, stop=True), sem=s["dmm"])
                    fpar = nfm % 2
                    if nfm >= 2:
                        p.wait("sync", s["fin"], nfm - 1)
                    p.dma("sync", x_fin[fpar].ap(), x_own[row0:row0 + 128, dc * 512:(dc + 1) * 512],
                          sem=s["fld"])
                    p.dma("sync", rs_fin[fpar].ap(),
                          wcout_sc[row0:row0 + 128, dc * 512:(dc + 1) * 512], sem=s["fld"])
                    p.wait("vector", s["fld"])
                    p.wait("vector", s["dmm"])
                    p.op("vector", lambda v, i=ps, x=x_fin[fpar].ap(),
                         o=o_fin[fpar].ap(): v.scalar_tensor_tensor(
                        o, i, 1.0, x, ALU.bypass, ALU.add))
                    p.op("vector", lambda v, a=o_fin[fpar].ap(), b=rs_fin[fpar].ap(),
                         o=x_fin[fpar].ap(): v.tensor_tensor(o, a, b, ALU.add), sem=s["fin"])
                    p.wait("sync", s["fin"])
                    p.dma("sync", out_own[row0:row0 + 128, dc * 512:(dc + 1) * 512],
                          x_fin[fpar].ap(), sem=s["out"])
                    nfm += 1

        p.wait("gpsimd", s["out"])

        block = es.enter_context(nc.Block())

        @block.sync
        def _(eng):
            for fn in p.sched["sync"]:
                fn(eng)

        @block.scalar
        def _(eng):
            for fn in p.sched["scalar"]:
                fn(eng)

        @block.vector
        def _(eng):
            for fn in p.sched["vector"]:
                fn(eng)

        @block.tensor
        def _(eng):
            for fn in p.sched["tensor"]:
                fn(eng)

        @block.gpsimd
        def _(eng):
            for fn in p.sched["gpsimd"]:
                fn(eng)

    _CACHE["nc"] = nc
    return nc


def kernel(L0_attn_hidden_states, L0_attn_w_qkv, L0_attn_w_c, L0_ffn_w_gate,
           L0_ffn_w_up, L0_ffn_w_down, L0_ffn_mlp_output_scale, L0_ffn_lr_A,
           L0_ffn_lr_B, L0_ffn_up_reorder_index, L0_ffn_down_reorder_index):
    x = np.asarray(L0_attn_hidden_states, dtype=np.float32)
    w_qkv = np.asarray(L0_attn_w_qkv, dtype=np.float32)
    w_c = np.asarray(L0_attn_w_c, dtype=np.float32)
    wg = np.asarray(L0_ffn_w_gate, dtype=np.float32)
    wu = np.asarray(L0_ffn_w_up, dtype=np.float32)
    wd = np.asarray(L0_ffn_w_down, dtype=np.float32)
    mscale = np.asarray(L0_ffn_mlp_output_scale, dtype=np.float32)
    lr_A = np.asarray(L0_ffn_lr_A, dtype=np.float32)
    lr_B = np.asarray(L0_ffn_lr_B, dtype=np.float32)
    uperm = np.asarray(L0_ffn_up_reorder_index, dtype=np.int64)
    dperm = np.asarray(L0_ffn_down_reorder_index, dtype=np.int64)

    x_eff = x[:, :, uperm]
    wqkv_eff = w_qkv[uperm, :]
    wc_eff = np.ascontiguousarray(w_c[:, uperm])
    wg_eff = wg[dperm][:, uperm]
    wu_eff = wu[dperm][:, uperm]
    wd_eff = wd[uperm][:, dperm]
    scale_eff = mscale[uperm]
    lrA_eff = np.ascontiguousarray(lr_A[uperm, :])
    lrB_eff = np.ascontiguousarray(lr_B[:, uperm])

    slopes = _alibi_slopes(32)
    nc = build_program()

    in_maps = []
    for c in range(8):
        b, g = c // 4, c % 4
        sl = slopes[4 * c:4 * c + 4]
        rr = np.arange(128, dtype=np.float32)
        cols = np.zeros((128, NHL * 8 * 32), np.float32)
        for h in range(NHL):
            for cc in range(8):
                for t in range(32):
                    cols[:, (h * 8 + cc) * 32 + t] = sl[h] * (128.0 * t + rr - 512.0 * cc)
        bq = np.zeros((2, NHL * 512), BF)
        qq = np.arange(512, dtype=np.float64)
        for h in range(NHL):
            bfull = (-float(sl[h]) * qq / SCAL).astype(np.float32)
            hi = bfull.astype(BF)
            lo = (bfull - hi.astype(np.float32)).astype(BF)
            bq[0, h * 512:(h + 1) * 512] = hi
            bq[1, h * 512:(h + 1) * 512] = lo
        uu = np.arange(896)
        msk = np.where(uu[None, :] >= (384 + np.arange(128)[:, None]),
                       np.float32(3e38), np.float32(-3e38)).astype(np.float32)
        in_maps.append({
            "x_own": np.ascontiguousarray(x_eff[b, 1024 * g:1024 * (g + 1)]),
            "wqkv": np.ascontiguousarray(
                np.hstack([wqkv_eff[:, 512 * c:512 * (c + 1)], wqkv_eff[:, 4096:4352]])),
            "wc": wc_eff,
            "wg_sh": np.ascontiguousarray(wg_eff[FS * c:FS * (c + 1)]),
            "wu_sh": np.ascontiguousarray(wu_eff[FS * c:FS * (c + 1)]),
            "wd_sh": np.ascontiguousarray(wd_eff[DS * c:DS * (c + 1)]),
            "scale_sh": np.ascontiguousarray(scale_eff[DS * c:DS * (c + 1)].reshape(4, 128).T),
            "lrA": lrA_eff,
            "lrB": lrB_eff,
            "aux_bias": cols,
            "aux_bq": bq,
            "aux_mask": msk,
            "aux_ones": np.ones((128, 128), BF),
        })

    res = run_bass_kernel_spmd(nc, in_maps, core_ids=list(range(8)))

    out_eff = np.empty((2, S, E), np.float32)
    for c in range(8):
        b, g = c // 4, c % 4
        out_eff[b, 1024 * g:1024 * (g + 1)] = res.results[c]["out_own"]
    out = np.empty_like(out_eff)
    out[:, :, uperm] = out_eff
    return out


# version 09
# speedup vs baseline: 1.0481x; 1.0481x over previous
import math
import numpy as np
import ml_dtypes
from contextlib import ExitStack

import concourse.bass as bass
import concourse.mybir as mybir
from concourse.bass_utils import run_bass_kernel_spmd
from concourse.aot_env import allow_ici_all_to_all

F32 = mybir.dt.float32
BF16 = mybir.dt.bfloat16
AF = mybir.ActivationFunctionType
ALU = mybir.AluOpType
AX = mybir.AxisListType
BF = ml_dtypes.bfloat16

S, E, F, R = 4096, 4096, 11008, 64
TOK = 1024
NHL = 4
QKV = NHL * 128 + 256
FS, FSP, DS = F // 8, 1408, E // 8
SCAL = float(128) ** -0.5
M_H = float(np.float32(1.5 * 2 ** 22))
M_I = float(np.float32(1.5 * 2 ** 23))
EPS = 1e-5
QW = 512

_CACHE = {}


def _alibi_slopes(n=32):
    cp2 = 2 ** int(math.floor(math.log2(n)))
    base = 2.0 ** (-2.0 ** (-(math.log2(cp2) - 3)))
    slopes = base ** np.arange(1, cp2 + 1, dtype=np.float64)
    if cp2 != n:
        eb = 2.0 ** (-2.0 ** (-(math.log2(2 * cp2) - 3)))
        rem = min(cp2, n - cp2)
        slopes = np.concatenate([slopes, eb ** np.arange(1, 1 + 2 * rem, 2, dtype=np.float64)])
    return slopes.astype(np.float32)


class Sem:
    def __init__(self, es, nc, name):
        self.h = es.enter_context(nc.semaphore(name))
        self.n = 0


class Prog:
    def __init__(self):
        self.sched = {k: [] for k in ("sync", "scalar", "vector", "tensor", "gpsimd")}

    def wait(self, eng, sem, val=None):
        v = sem.n if val is None else val
        if v > 0:
            self.sched[eng].append(lambda e, s=sem.h, vv=v: e.wait_ge(s, vv))

    def op(self, eng, fn, sem=None, k=1):
        if sem is None:
            self.sched[eng].append(fn)
        else:
            sem.n += k
            self.sched[eng].append(lambda e, f=fn, s=sem.h, kk=k: f(e).then_inc(s, kk))

    def dma(self, eng, out, in_, sem=None, transpose=False):
        self.op(eng, lambda e, o=out, i=in_, t=transpose: e.dma_start(out=o, in_=i, transpose=t),
                sem=sem, k=16)


def build_program():
    if "nc" in _CACHE:
        return _CACHE["nc"]
    nc = bass.Bass("TRN2", target_bir_lowering=False)

    x_own = nc.dram_tensor("x_own", [TOK, E], F32, kind="ExternalInput").ap()
    wqkv = nc.dram_tensor("wqkv", [E, QKV], F32, kind="ExternalInput").ap()
    wc = nc.dram_tensor("wc", [E, E], F32, kind="ExternalInput").ap()
    wg_sh = nc.dram_tensor("wg_sh", [FS, E], F32, kind="ExternalInput").ap()
    wu_sh = nc.dram_tensor("wu_sh", [FS, E], F32, kind="ExternalInput").ap()
    wd_sh = nc.dram_tensor("wd_sh", [DS, F], F32, kind="ExternalInput").ap()
    scale_sh = nc.dram_tensor("scale_sh", [128, 4], F32, kind="ExternalInput").ap()
    lrA = nc.dram_tensor("lrA", [E, R], F32, kind="ExternalInput").ap()
    lrB = nc.dram_tensor("lrB", [R, E], F32, kind="ExternalInput").ap()
    aux_bias = nc.dram_tensor("aux_bias", [128, NHL * 8 * 32], F32, kind="ExternalInput").ap()
    aux_bq = nc.dram_tensor("aux_bq", [2, NHL * 512], BF16, kind="ExternalInput").ap()
    aux_mask = nc.dram_tensor("aux_mask", [128, 896], F32, kind="ExternalInput").ap()
    aux_ones = nc.dram_tensor("aux_ones", [128, 128], BF16, kind="ExternalInput").ap()
    out_own = nc.dram_tensor("out_own", [TOK, E], F32, kind="ExternalOutput").ap()

    xnown_sc = nc.dram_tensor("xnown_sc", [TOK, E], BF16).ap()
    agout_x = nc.dram_tensor("agout_x", [8 * TOK, E], BF16).ap()
    xq_sc = nc.dram_tensor("xq_sc", [TOK, E], BF16).ap()
    qkT_sc = nc.dram_tensor("qkT_sc", [2 * QKV, S], BF16).ap()
    hq_sc = nc.dram_tensor("hq_sc", [TOK, F], BF16).ap()
    wcout_sc = nc.dram_tensor("wcout_sc", [TOK, E], BF16).ap()
    wgq_sc = nc.dram_tensor("wgq_sc", [FSP, E], BF16).ap()
    wuq_sc = nc.dram_tensor("wuq_sc", [FSP, E], BF16).ap()
    wdq_sc = nc.dram_tensor("wdq_sc", [DS, F], BF16).ap()
    agin_g = nc.dram_tensor("agin_g", [E, FSP], BF16).ap()
    agin_u = nc.dram_tensor("agin_u", [E, FSP], BF16).ap()
    agin_d = nc.dram_tensor("agin_d", [F, DS], BF16).ap()
    agout_g = nc.dram_tensor("agout_g", [8 * E, FSP], BF16).ap()
    agout_u = nc.dram_tensor("agout_u", [8 * E, FSP], BF16).ap()
    agout_d = nc.dram_tensor("agout_d", [8 * F, DS], BF16).ap()
    a2a_in = nc.dram_tensor("a2a_in", [8 * 512, TOK], BF16).ap()
    a2a_out = nc.dram_tensor("a2a_out", [8 * 512, TOK], BF16).ap()
    wc_bf = nc.dram_tensor("wc_bf", [E, E], BF16).ap()

    G8 = [list(range(8))]
    p = Prog()
    SBUF_LIM = 212863

    with ExitStack() as es:
        ARENA = 151552
        arena_base = [None]

        def alloc(name, shape, dtype, at=None, tdest=False):
            if at is None:
                return nc.alloc_sbuf_tensor(name, shape, dtype)
            itemsz = 2 if dtype == BF16 else 4
            per_part = int(np.prod(shape[1:])) * itemsz
            o = arena_base[0] + (at - (1 << 20))
            assert (at - (1 << 20)) + per_part <= ARENA, (name, at - (1 << 20) + per_part)
            return nc.alloc_sbuf_tensor_at(name, shape, dtype, offset=o)

        q_a = alloc("q_a", [128, QW], F32)
        q_c2 = alloc("q_c2", [128, QW], F32)
        q_c4 = alloc("q_c4", [128, QW], F32)
        q_t1 = alloc("q_t1", [128, QW], F32)
        q_in = alloc("q_in", [128, QW], F32)
        q_r1 = alloc("q_r1", [128, QW], BF16)
        q_r2 = alloc("q_r2", [128, QW], BF16)
        q_r3 = alloc("q_r3", [128, QW], BF16)
        q_qa = alloc("q_qa", [128, QW], BF16)
        q_out = alloc("q_out", [128, QW], BF16)
        q_bm = alloc("q_bm", [128, QW // 16], F32)
        q_r6 = alloc("q_r6", [128, QW // 16], F32)
        q_bm6 = alloc("q_bm6", [128, QW // 16], F32)
        bias_sb = alloc("bias_sb", [128, NHL * 8 * 32], F32)
        bq_sb = alloc("bq_sb", [2, NHL * 512], BF16)
        mask_sb = alloc("mask_sb", [128, 896], F32)
        ones_sb = alloc("ones_sb", [128, 128], BF16)
        t1T_sb = alloc("t1T_sb", [R, TOK], BF16)
        stat_a = alloc("stat_a", [128, 8], F32)
        stat_b = alloc("stat_b", [128, 8], F32)
        st_mean = alloc("st_mean", [128, 1], F32)
        st_var = alloc("st_var", [128, 1], F32)
        st_rstd = alloc("st_rstd", [128, 1], F32)
        st_nb = alloc("st_nb", [128, 1], F32)
        ev_bf = [alloc(f"ev_bf{i}", [128, 512], BF16) for i in range(2)]
        scale_sb = alloc("scale_sb", [128, 4], F32)
        rec_sb = alloc("rec_sb", [128, 512], F32)
        lnr_sb = alloc("lnr_sb", [128, 512], F32)
        attn_bf = alloc("attn_bf", [128, 512], BF16)
        lrA_bf = alloc("lrA_bf", [128, 32 * R], BF16)
        lrB_bf = alloc("lrB_bf", [R, E], BF16)
        x_fin = [alloc(f"x_fin{i}", [128, 512], F32) for i in range(2)]
        rs_fin = [alloc(f"rs_fin{i}", [128, 512], BF16) for i in range(2)]
        o_fin = [alloc(f"o_fin{i}", [128, 512], F32) for i in range(2)]
        _arena_t = nc.alloc_sbuf_tensor("overlay_arena", [128, ARENA], mybir.dt.uint8)
        arena_base[0] = nc.lookup_mloc(_arena_t).addr
        OV = 1 << 20

        xb = [alloc(f"xb{i}", [128, E], F32, at=OV + i * 16384) for i in range(2)]
        wqkv_bf = alloc("wqkv_bf", [128, 32 * QKV], BF16, at=OV + 32768)
        rhs_buf = [alloc(f"rhs{i}", [128, 32 * 512], BF16,
                         at=(OV + 0 if i == 0 else OV + 81920)) for i in range(2)]
        kT2 = alloc("kT2", [128, 2 * S], BF16, at=OV)
        v2 = alloc("v2", [128, 2 * S], BF16, at=OV + 16384)
        qT_buf = [alloc(f"qT{i}", [128, 512], BF16, at=OV + 32768 + i * 1024) for i in range(2)]
        probs_sb = alloc("probs_sb", [128, 8 * 512], BF16, at=OV + 34816)
        wq_in = [alloc(f"wq_in{i}", [128, QW], F32, at=OV + 131072 + i * 2048) for i in range(2)]
        wq_tr = [alloc(f"wq_tr{i}", [128, FSP], BF16, at=OV + 135168 + i * 2816) for i in range(2)]
        a2a_res = alloc("a2a_res", [128, 8 * 32 * 128], BF16, at=OV)
        wc_rhs = [alloc(f"wc_rhs{i}", [128, 32 * 512], BF16, at=OV + 65536 + i * 32768) for i in range(2)]
        xqT_sb = alloc("xqT_sb", [128, 32 * TOK], BF16, at=OV)
        wgu_buf = [alloc(f"wgu{i}", [128, 32 * 512], BF16, at=OV + 65536 + i * 32768) for i in range(2)]
        gs_sb = alloc("gs_sb", [128, 8 * 512], F32, at=OV + 131072)
        hqT_sb = alloc("hqT_sb", [128, 86 * 512], BF16, at=OV)
        wd_buf = [alloc(f"wd_buf{i}", [128, 16 * 512], BF16, at=OV + 90112 + i * 16384) for i in range(2)]

        psA = es.enter_context(nc.psum_tensor("psA", [128, 2048], F32))
        psB = es.enter_context(nc.psum_tensor("psB", [128, 2048], F32))

        sn = ("aux ldx stat app xnw xqv xqw agx wqkvl rhs mm ev qkw lora lrb scl "
              "kv qt st exp msk pv rec evac a2w a2a "
              "wqi wqv wqw wqt agw ag0 ag1 ag2 "
              "cw ca cmm cev cwr "
              "xqt gw uw gmm umm sil hmul hq hqw "
              "hqt dw dgrp dmm fin fld out vsq ssq cwp").split()
        s = {n: Sem(es, nc, "s_" + n) for n in sn}

        def emit_quant(src_ap, w, dst_ap, pp_scale=None, sem_out=None):
            nb = w // 16
            src3 = src_ap.rearrange("p (b k) -> p b k", k=16)
            p.op("vector", lambda v, ss=src3, n=nb: v.tensor_reduce(
                q_bm.ap()[:, :n], ss, AX.X, ALU.max, apply_absolute_value=True),
                sem=s["vsq"])
            p.wait("vector", s["vsq"])
            p.op("vector", lambda v, n=nb: v.tensor_scalar(
                q_bm.ap()[:, :n], q_bm.ap()[:, :n], 1e-30, None, ALU.max),
                sem=s["vsq"])
            p.wait("vector", s["vsq"])
            p.op("vector", lambda v, n=nb: v.reciprocal(q_r6.ap()[:, :n], q_bm.ap()[:, :n]),
                 sem=s["vsq"])
            if pp_scale is None:
                p.op("vector", lambda v, n=nb: v.tensor_scalar(
                    q_bm6.ap()[:, :n], q_bm.ap()[:, :n], 1.0 / 6.0, None, ALU.mult))
            else:
                p.op("vector", lambda v, n=nb, pa=pp_scale: v.tensor_scalar(
                    q_bm6.ap()[:, :n], q_bm.ap()[:, :n], 1.0 / 6.0, pa, ALU.mult, ALU.mult))
            p.wait("vector", s["vsq"])
            r6b = q_r6.ap()[:, :nb].unsqueeze(2).broadcast_to([128, nb, 16])
            bm6b = q_bm6.ap()[:, :nb].unsqueeze(2).broadcast_to([128, nb, 16])
            a3 = q_a.ap()[:, :w].rearrange("p (b k) -> p b k", k=16)
            p.op("vector", lambda v, ss=src3, r=r6b, a=a3: v.scalar_tensor_tensor(
                a, ss, 6.0, r, ALU.mult, ALU.mult))
            p.op("vector", lambda v, w=w: v.tensor_scalar(
                q_c2.ap()[:, :w], q_a.ap()[:, :w], -2.0, 2.0, ALU.max, ALU.min))
            p.op("vector", lambda v, w=w: v.tensor_scalar(
                q_c4.ap()[:, :w], q_a.ap()[:, :w], -4.0, 4.0, ALU.max, ALU.min))
            p.op("vector", lambda v, w=w: v.tensor_scalar(
                q_r1.ap()[:, :w], q_c2.ap()[:, :w], M_H, -M_H, ALU.add, ALU.add))
            p.op("vector", lambda v, w=w: v.tensor_sub(
                q_t1.ap()[:, :w], q_c4.ap()[:, :w], q_c2.ap()[:, :w]))
            p.op("vector", lambda v, w=w: v.tensor_scalar(
                q_r2.ap()[:, :w], q_t1.ap()[:, :w], M_I, -M_I, ALU.add, ALU.add))
            p.op("vector", lambda v, w=w: v.tensor_sub(
                q_c2.ap()[:, :w], q_a.ap()[:, :w], q_c4.ap()[:, :w]))
            p.op("vector", lambda v, w=w: v.tensor_scalar(
                q_t1.ap()[:, :w], q_c2.ap()[:, :w], 0.5, M_I, ALU.mult, ALU.add))
            p.op("vector", lambda v, w=w: v.tensor_scalar(
                q_r3.ap()[:, :w], q_t1.ap()[:, :w], -M_I, 2.0, ALU.add, ALU.mult))
            p.op("vector", lambda v, w=w: v.tensor_add(
                q_qa.ap()[:, :w], q_r1.ap()[:, :w], q_r2.ap()[:, :w]))
            p.op("vector", lambda v, w=w: v.tensor_add(
                q_r1.ap()[:, :w], q_qa.ap()[:, :w], q_r3.ap()[:, :w]))
            d3 = dst_ap.rearrange("p (b k) -> p b k", k=16)
            r13 = q_r1.ap()[:, :w].rearrange("p (b k) -> p b k", k=16)
            p.op("vector", lambda v, d=d3, r=r13, m=bm6b: v.tensor_tensor(d, r, m, ALU.mult),
                 sem=sem_out)

        p.dma("sync", bias_sb.ap(), aux_bias, sem=s["aux"])
        p.dma("sync", bq_sb.ap(), aux_bq, sem=s["aux"])
        p.dma("sync", mask_sb.ap(), aux_mask, sem=s["aux"])
        p.dma("sync", ones_sb.ap(), aux_ones, sem=s["aux"])
        p.dma("sync", scale_sb.ap(), scale_sh, sem=s["scl"])
        p.dma("gpsimd", wqkv_bf.ap().rearrange("p (kk n) -> p kk n", kk=32),
              wqkv.rearrange("(kk p) n -> p kk n", p=128), sem=s["wqkvl"])
        p.dma("gpsimd", lrA_bf.ap().rearrange("p (kk r) -> p kk r", kk=32),
              lrA.rearrange("(kk p) r -> p kk r", p=128), sem=s["lrb"])
        p.dma("gpsimd", lrB_bf.ap(), lrB, sem=s["lrb"])
        for i in range(8):
            p.dma("gpsimd", wc_bf[i * 512:(i + 1) * 512, :],
                  wc[i * 512:(i + 1) * 512, :], sem=s["cwp"])

        NCK = E // QW
        for i in range(8):
            par = i % 2
            if i >= 2:
                p.wait("sync", s["app"], 2 * NCK * (i - 1))
            p.dma("sync", xb[par].ap(), x_own[i * 128:(i + 1) * 128, :], sem=s["ldx"])
            p.wait("vector", s["ldx"], (i + 1) * 16)
            for k in range(NCK):
                ck = xb[par].ap()[:, k * QW:(k + 1) * QW]
                last = (k == NCK - 1)
                p.op("vector", lambda v, c=ck, kk=k: v.scalar_tensor_tensor(
                    q_t1.ap()[:, :QW], c, 1.0, c, ALU.bypass, ALU.mult,
                    accum_out=stat_b.ap()[:, kk:kk + 1]),
                    sem=(s["vsq"] if last else None))
                p.op("vector", lambda v, c=ck, kk=k: v.tensor_reduce(
                    stat_a.ap()[:, kk:kk + 1], c, AX.X, ALU.add),
                    sem=(s["vsq"] if last else None))
            p.wait("vector", s["vsq"])
            p.op("vector", lambda v: v.tensor_reduce(st_mean.ap(), stat_a.ap()[:, 0:NCK], AX.X, ALU.add), sem=s["vsq"])
            p.op("vector", lambda v: v.tensor_reduce(st_var.ap(), stat_b.ap()[:, 0:NCK], AX.X, ALU.add), sem=s["vsq"])
            p.wait("vector", s["vsq"])
            p.op("vector", lambda v: v.tensor_scalar(st_mean.ap(), st_mean.ap(), 1.0 / E, None, ALU.mult), sem=s["vsq"])
            p.op("vector", lambda v: v.tensor_scalar(st_var.ap(), st_var.ap(), 1.0 / E, None, ALU.mult), sem=s["vsq"])
            p.wait("vector", s["vsq"])
            p.op("vector", lambda v: v.tensor_scalar(
                q_t1.ap()[:, 0:1], st_mean.ap(), st_mean.ap()[:, 0:1], None, ALU.mult), sem=s["vsq"])
            p.wait("vector", s["vsq"])
            p.op("vector", lambda v: v.tensor_sub(
                st_var.ap(), st_var.ap(), q_t1.ap()[:, 0:1]), sem=s["vsq"])
            p.wait("vector", s["vsq"])
            p.op("vector", lambda v: v.tensor_scalar(
                st_var.ap(), st_var.ap(), EPS, None, ALU.add), sem=s["stat"])
            p.wait("scalar", s["stat"])
            p.op("scalar", lambda sc: sc.activation(st_rstd.ap(), st_var.ap(), AF.Ln), sem=s["ssq"])
            p.wait("scalar", s["ssq"])
            p.op("scalar", lambda sc: sc.activation(st_rstd.ap(), st_rstd.ap(), AF.Exp, scale=-0.5),
                 sem=s["stat"])
            p.wait("vector", s["stat"])
            p.op("vector", lambda v: v.tensor_scalar(
                st_nb.ap(), st_mean.ap(), st_rstd.ap()[:, 0:1], -1.0, ALU.mult, ALU.mult),
                sem=s["stat"])
            p.wait("vector", s["stat"])
            p.wait("scalar", s["stat"])
            nbb = st_nb.ap()[:, 0:1].broadcast_to([128, QW])
            for k in range(NCK):
                ck = xb[par].ap()[:, k * QW:(k + 1) * QW]
                p.wait("vector", s["xqv"])
                p.op("vector", lambda v, c=ck, nb=nbb: v.scalar_tensor_tensor(
                    q_in.ap()[:, :QW], c, st_rstd.ap()[:, 0:1], nb, ALU.mult, ALU.add),
                    sem=s["app"])
                p.wait("vector", s["xnw"])
                p.op("vector", lambda v, c=ck, nb=nbb: v.scalar_tensor_tensor(
                    q_qa.ap()[:, :QW], c, st_rstd.ap()[:, 0:1], nb, ALU.mult, ALU.add),
                    sem=s["app"])
                p.wait("scalar", s["app"])
                p.dma("scalar", xnown_sc[i * 128:(i + 1) * 128, k * QW:(k + 1) * QW],
                      q_qa.ap()[:, :QW], sem=s["xnw"])
                p.wait("vector", s["xqw"])
                p.wait("vector", s["xnw"])
                emit_quant(q_in.ap()[:, :QW], QW, q_out.ap()[:, :QW], sem_out=s["xqv"])
                if k == NCK - 1:
                    p.op("vector", lambda v: v.tensor_copy(
                        q_out.ap()[:, QW - 128:QW], q_in.ap()[:, QW - 128:QW]), sem=s["xqv"])
                p.wait("sync", s["xqv"])
                p.dma("sync", xq_sc[i * 128:(i + 1) * 128, k * QW:(k + 1) * QW],
                      q_out.ap()[:, :QW], sem=s["xqw"])
        p.wait("gpsimd", s["xnw"])
        p.op("gpsimd", lambda g: g.collective_compute(
            "AllGather", ALU.bypass, replica_groups=G8, ins=[xnown_sc], outs=[agout_x]),
            sem=s["agx"])

        wq_units = []

        def mk_unit(srcd, dstd, r0, rh, srh, colk, w, pp, pt):
            def load(par):
                p.dma("scalar", wq_in[par].ap()[0:rh, :w], srcd[r0:r0 + rh, colk:colk + w],
                      sem=s["wqi"])

            def quant(par):
                p.wait("vector", s["wqi"])
                p.wait("vector", s["xqw"])
                p.wait("vector", s["wqw"])
                if pp is not None:
                    p.wait("vector", s["scl"], 16)
                emit_quant(wq_in[par].ap()[0:128, :w], w, q_out.ap()[0:128, :w],
                           pp_scale=pp, sem_out=s["wqv"])
                if pt:
                    if pp is None:
                        p.op("vector", lambda v, ww=w: v.tensor_copy(
                            q_out.ap()[:, ww - 128:ww], wq_in[par].ap()[:, ww - 128:ww]),
                            sem=s["wqv"])
                    else:
                        p.op("vector", lambda v, ww=w, pa=pp: v.tensor_scalar(
                            q_out.ap()[:, ww - 128:ww], wq_in[par].ap()[:, ww - 128:ww],
                            pa, None, ALU.mult), sem=s["wqv"])
                p.wait("scalar", s["wqv"])
                p.dma("scalar", dstd[r0:r0 + srh, colk:colk + w], q_out.ap()[0:srh, :w],
                      sem=s["wqw"])
            return load, quant

        for (srcd, dstd, nr, pp_on, fax) in ((wg_sh, wgq_sc, FS, False, False),
                                             (wu_sh, wuq_sc, FS, False, False),
                                             (wd_sh, wdq_sc, DS, True, True)):
            ncols = F if fax else E
            r0 = 0
            while r0 < nr:
                rh = min(128, nr - r0)
                srh = 128 if (nr == FS and r0 + 128 == FSP) else rh
                pp = scale_sb.ap()[:, r0 // 128:r0 // 128 + 1] if pp_on else None
                for colk in range(0, ncols, QW):
                    w = min(QW, ncols - colk)
                    pt = (colk + w == ncols)
                    wq_units.append(mk_unit(srcd, dstd, r0, rh, srh, colk, w, pp, pt))
                r0 += 128
        wq_state = {"i": 0, "loaded": 0}
        wqv_after = []

        def wq_load_next():
            j = wq_state["loaded"]
            if j < len(wq_units):
                if j >= 2:
                    p.wait("scalar", s["wqv"], wqv_after[j - 2])
                wq_units[j][0](j % 2)
                wq_state["loaded"] += 1

        def drain_wq(n):
            for _ in range(n):
                i = wq_state["i"]
                if i >= len(wq_units):
                    return
                if wq_state["loaded"] == i:
                    wq_load_next()
                wq_load_next()
                wq_units[i][1](i % 2)
                wqv_after.append(s["wqv"].n)
                wq_state["i"] += 1

        p.wait("tensor", s["wqkvl"])
        p.wait("sync", s["agx"], 1)
        p.wait("sync", s["app"])
        nmm = 0
        mm_after_chunk = []
        for ch in range(16):
            bb, tc = ch // 8, ch % 8
            par = ch % 2
            if ch >= 2:
                p.wait("sync", s["mm"], mm_after_chunk[ch - 2])
            for kk in range(32):
                p.dma("sync", rhs_buf[par].ap()[:, kk * 512:(kk + 1) * 512],
                      agout_x[bb * S + tc * 512:bb * S + (tc + 1) * 512, kk * 128:(kk + 1) * 128],
                      sem=s["rhs"], transpose=True)
            p.wait("tensor", s["rhs"])
            for m in range(6):
                slot = nmm % 4
                ps = psA.ap()[:, slot * 512:(slot + 1) * 512]
                if nmm >= 4:
                    p.wait("tensor", s["ev"], nmm - 3)
                for kk in range(32):
                    p.op("tensor", lambda t, o=ps,
                         l=wqkv_bf.ap()[:, kk * QKV + m * 128:kk * QKV + (m + 1) * 128],
                         r=rhs_buf[par].ap()[:, kk * 512:(kk + 1) * 512],
                         st=(kk == 0), sp=(kk == 31): t.matmul(o, l, r, start=st, stop=sp),
                         sem=(s["mm"] if kk == 31 else None))
                p.wait("vector", s["mm"])
                if nmm >= 2:
                    p.wait("vector", s["qkw"], (nmm - 1) * 16)
                evp = ev_bf[nmm % 2].ap()
                p.op("vector", lambda v, o=evp, i=ps: v.tensor_copy(o, i), sem=s["ev"])
                p.wait("scalar", s["ev"])
                p.dma("scalar", qkT_sc[bb * QKV + m * 128:bb * QKV + (m + 1) * 128,
                                       tc * 512:(tc + 1) * 512], evp, sem=s["qkw"])
                nmm += 1
            mm_after_chunk.append(s["mm"].n)
            drain_wq(4)
        p.wait("tensor", s["lrb"], 32)
        for tc in range(2):
            par = tc % 2
            p.wait("sync", s["mm"], mm_after_chunk[14 + tc])
            for kk in range(32):
                p.dma("sync", rhs_buf[par].ap()[:, kk * 512:(kk + 1) * 512],
                      xnown_sc[tc * 512:(tc + 1) * 512, kk * 128:(kk + 1) * 128],
                      sem=s["rhs"], transpose=True)
            p.wait("tensor", s["rhs"])
            ps = psB.ap()[0:R, tc * 512:(tc + 1) * 512]
            for kk in range(32):
                p.op("tensor", lambda t, o=ps, l=lrA_bf.ap()[:, kk * R:(kk + 1) * R],
                     r=rhs_buf[par].ap()[:, kk * 512:(kk + 1) * 512],
                     st=(kk == 0), sp=(kk == 31): t.matmul(o, l, r, start=st, stop=sp),
                     sem=(s["mm"] if kk == 31 else None))
            p.wait("vector", s["mm"])
            p.op("vector", lambda v, o=t1T_sb.ap()[:, tc * 512:(tc + 1) * 512], i=ps:
                 v.tensor_copy(o, i), sem=s["lora"])

        p.wait("sync", s["qkw"])
        p.wait("sync", s["lora"])
        for bb in range(2):
            p.dma("sync", kT2.ap()[:, bb * S:(bb + 1) * S],
                  qkT_sc[bb * QKV + 512:bb * QKV + 640, :], sem=s["kv"])
            for t in range(32):
                p.dma("sync", v2.ap()[:, bb * S + t * 128:bb * S + (t + 1) * 128],
                      qkT_sc[bb * QKV + 640:bb * QKV + 768, t * 128:(t + 1) * 128],
                      sem=s["kv"], transpose=True)
        p.wait("tensor", s["kv"])
        p.wait("tensor", s["ev"])
        p.wait("tensor", s["lora"])
        for eng in ("tensor", "scalar", "vector"):
            p.wait(eng, s["aux"])

        it = 0
        st_after_it = []
        for bb in range(2):
            for h in range(NHL):
                for c in range(8):
                    nk = 4 * (c + 1)
                    par = it % 2
                    apar = it % 2
                    if it >= 2:
                        p.wait("sync", s["st"], st_after_it[it - 2])
                    p.dma("sync", qT_buf[par].ap(),
                          qkT_sc[bb * QKV + h * 128:bb * QKV + (h + 1) * 128,
                                 c * 512:(c + 1) * 512], sem=s["qt"])
                    p.wait("tensor", s["qt"])
                    attn_ps = psB.ap()[:, apar * 512:(apar + 1) * 512]
                    rows_ps = psB.ap()[:, (2 + apar) * 512:(3 + apar) * 512]
                    if it >= 2:
                        p.wait("tensor", s["evac"], it - 1)
                    bias_base = (h * 8 + c) * 32
                    ngrp = (nk + 3) // 4
                    for grp in range(ngrp):
                        tlist = list(range(grp * 4, min(grp * 4 + 4, nk)))
                        if grp >= 1:
                            p.wait("tensor", s["exp"])
                        if grp >= 2 or (grp == 0 and it >= 1):
                            p.wait("scalar", s["pv"])
                        for t in tlist:
                            slot = t % 4
                            stp = psA.ap()[:, slot * 512:(slot + 1) * 512]
                            p.op("tensor", lambda tt, o=stp,
                                 l=kT2.ap()[:, bb * S + t * 128:bb * S + (t + 1) * 128],
                                 r=qT_buf[par].ap(): tt.matmul(o, l, r, start=True, stop=False))
                            p.op("tensor", lambda tt, o=stp, l=ones_sb.ap()[0:2, :],
                                 r=bq_sb.ap()[0:2, h * 512:(h + 1) * 512]:
                                 tt.matmul(o, l, r, start=False, stop=True), sem=s["st"])
                        msk_base = s["msk"].n
                        for t in tlist:
                            if t < nk - 4:
                                continue
                            slot = t % 4
                            stp = psA.ap()[:, slot * 512:(slot + 1) * 512]
                            d = t - (nk - 4)
                            p.wait("vector", s["st"], s["st"].n - (tlist[-1] - t))
                            p.op("vector", lambda v, o=stp,
                                 m=mask_sb.ap()[:, 384 - 128 * d:896 - 128 * d]:
                                 v.tensor_tensor(o, o, m, ALU.min), sem=s["msk"])
                        for t in tlist:
                            slot = t % 4
                            stp = psA.ap()[:, slot * 512:(slot + 1) * 512]
                            prb = probs_sb.ap()[:, (t % 8) * 512:(t % 8) * 512 + 512]
                            p.wait("scalar", s["st"], s["st"].n - (tlist[-1] - t))
                            if t >= nk - 4:
                                p.wait("scalar", s["msk"], msk_base + (t - (nk - 4) + 1))
                            p.op("scalar", lambda sc, o=prb, i=stp,
                                 ba=bias_sb.ap()[:, bias_base + t:bias_base + t + 1]:
                                 sc.activation(o, i, AF.Exp, bias=ba, scale=SCAL), sem=s["exp"])
                        p.wait("tensor", s["exp"])
                        for t in tlist:
                            prb = probs_sb.ap()[:, (t % 8) * 512:(t % 8) * 512 + 512]
                            p.op("tensor", lambda tt, o=attn_ps,
                                 l=v2.ap()[:, bb * S + t * 128:bb * S + (t + 1) * 128],
                                 r=prb, st=(t == 0), sp=(t == nk - 1):
                                 tt.matmul(o, l, r, start=st, stop=sp), sem=s["pv"])
                            p.op("tensor", lambda tt, o=rows_ps, l=ones_sb.ap(), r=prb,
                                 st=(t == 0), sp=(t == nk - 1):
                                 tt.matmul(o, l, r, start=st, stop=sp), sem=s["pv"])
                    st_after_it.append(s["st"].n)
                    p.wait("scalar", s["pv"])
                    if it >= 1:
                        p.wait("scalar", s["evac"], it)
                    p.op("scalar", lambda sc, i=rows_ps: sc.activation(lnr_sb.ap(), i, AF.Ln))
                    p.op("scalar", lambda sc: sc.activation(rec_sb.ap(), lnr_sb.ap(), AF.Exp,
                                                            scale=-1.0), sem=s["rec"])
                    p.wait("vector", s["rec"])
                    if it >= 1:
                        p.wait("vector", s["a2w"], it * 16)
                    p.op("vector", lambda v, i=attn_ps: v.tensor_tensor(
                        attn_bf.ap(), i, rec_sb.ap(), ALU.mult), sem=s["evac"])
                    p.wait("scalar", s["evac"])
                    j = bb * 4 + c // 2
                    p.dma("scalar", a2a_in[j * 512 + h * 128:j * 512 + (h + 1) * 128,
                                           (c % 2) * 512:(c % 2) * 512 + 512],
                          attn_bf.ap(), sem=s["a2w"])
                    it += 1
                    drain_wq(2)

        p.wait("gpsimd", s["a2w"])

        def _a2a_op(g):
            with allow_ici_all_to_all():
                return g.collective_compute(
                    "AllToAll", ALU.bypass, replica_groups=G8, ins=[a2a_in], outs=[a2a_out])
        p.op("gpsimd", _a2a_op, sem=s["a2a"])

        p.wait("sync", s["wqw"], 176 * 16)
        jj = 0
        for (src_, agi, njt, wrow) in ((wgq_sc, agin_g, 32, FSP), (wuq_sc, agin_u, 32, FSP)):
            for j in range(njt):
                par = jj % 2
                if jj >= 2:
                    p.wait("sync", s["agw"], (jj - 1) * 16)
                p.dma("sync", wq_tr[par].ap()[:, :wrow], src_[:, j * 128:(j + 1) * 128],
                      sem=s["wqt"], transpose=True)
                p.wait("scalar", s["wqt"])
                p.dma("scalar", agi[j * 128:(j + 1) * 128, :], wq_tr[par].ap()[:, :wrow],
                      sem=s["agw"])
                jj += 1
        p.wait("gpsimd", s["agw"], 64 * 16)
        p.op("gpsimd", lambda g: g.collective_compute(
            "AllGather", ALU.bypass, replica_groups=G8,
            ins=[agin_g], outs=[agout_g]), sem=s["ag0"])
        p.op("gpsimd", lambda g: g.collective_compute(
            "AllGather", ALU.bypass, replica_groups=G8,
            ins=[agin_u], outs=[agout_u]), sem=s["ag1"])

        p.wait("sync", s["a2a"], 1)
        p.wait("sync", s["cwp"])

        def load_a2a_tt(tt):
            for kk in range(32):
                p.dma("sync", a2a_res.ap()[:, (tt * 32 + kk) * 128:(tt * 32 + kk + 1) * 128],
                      a2a_out[kk * 128:(kk + 1) * 128, tt * 128:(tt + 1) * 128],
                      sem=s["ca"])

        def load_wc_dc(dc, par):
            for kk in range(32):
                p.dma("sync", wc_rhs[par].ap()[:, kk * 512:(kk + 1) * 512],
                      wc_bf[dc * 512:(dc + 1) * 512, kk * 128:(kk + 1) * 128],
                      sem=s["cw"], transpose=True)

        load_a2a_tt(0)
        load_wc_dc(0, 0)
        load_wc_dc(1, 1)
        for tt in range(1, 8):
            load_a2a_tt(tt)
        ncm = 0
        for dc in range(8):
            par = dc % 2
            p.wait("tensor", s["cw"], (dc + 1) * 32 * 16)
            for tt in range(8):
                p.wait("tensor", s["ca"], (tt + 1) * 32 * 16)
                slot = ncm % 4
                ps = psB.ap()[:, slot * 512:(slot + 1) * 512]
                if ncm >= 4:
                    p.wait("tensor", s["cev"], ncm - 3)
                for kk in range(32):
                    p.op("tensor", lambda t, o=ps,
                         l=a2a_res.ap()[:, (tt * 32 + kk) * 128:(tt * 32 + kk + 1) * 128],
                         r=wc_rhs[par].ap()[:, kk * 512:(kk + 1) * 512],
                         st=(kk == 0), sp=(kk == 31): t.matmul(o, l, r, start=st, stop=sp),
                         sem=(s["cmm"] if kk == 31 else None))
                p.wait("scalar", s["cmm"])
                if ncm >= 2:
                    p.wait("scalar", s["cwr"], (ncm - 1) * 16)
                evp = ev_bf[ncm % 2].ap()
                p.op("scalar", lambda sc, o=evp, i=ps: sc.activation(o, i, AF.Copy), sem=s["cev"])
                p.wait("scalar", s["cev"])
                p.dma("scalar", wcout_sc[tt * 128:(tt + 1) * 128, dc * 512:(dc + 1) * 512],
                      evp, sem=s["cwr"])
                ncm += 1
            p.wait("sync", s["cwr"])
            p.wait("tensor", s["cwr"])
            if dc + 2 < 8:
                p.wait("sync", s["cmm"], (dc + 1) * 8)
                load_wc_dc(dc + 2, par)
            drain_wq(11)
        drain_wq(10 ** 9)

        p.wait("sync", s["wqw"])
        jj2 = 0
        for j in range(86):
            par = jj2 % 2
            if jj2 >= 2:
                p.wait("sync", s["agw"], (64 + jj2 - 1) * 16)
            p.dma("sync", wq_tr[par].ap()[:, :DS], wdq_sc[:, j * 128:(j + 1) * 128],
                  sem=s["wqt"], transpose=True)
            p.wait("scalar", s["wqt"])
            p.dma("scalar", agin_d[j * 128:(j + 1) * 128, :], wq_tr[par].ap()[:, :DS],
                  sem=s["agw"])
            jj2 += 1
        p.wait("gpsimd", s["agw"])
        p.op("gpsimd", lambda g: g.collective_compute(
            "AllGather", ALU.bypass, replica_groups=G8, ins=[agin_d], outs=[agout_d]),
            sem=s["ag2"])

        p.wait("sync", s["cmm"])
        p.wait("sync", s["agw"])
        for j in range(32):
            p.dma("sync", xqT_sb.ap()[:, j * TOK:(j + 1) * TOK], xq_sc[:, j * 128:(j + 1) * 128],
                  sem=s["xqt"], transpose=True)
        p.wait("tensor", s["xqt"])
        p.wait("tensor", s["ag0"], 1)
        p.wait("tensor", s["ag1"], 1)
        p.wait("sync", s["ag0"], 1)
        p.wait("sync", s["ag1"], 1)
        p.wait("gpsimd", s["cmm"])
        p.wait("scalar", s["cmm"])
        p.wait("scalar", s["agw"])
        p.wait("vector", s["wqw"])
        chunks = [(r, o, w) for r in range(8) for (o, w) in ((0, 512), (512, 512), (1024, 352))]
        ng = 0
        for (r, coff, cw) in chunks:
            if ng >= 1:
                p.wait("sync", s["gmm"], 8 * ng)
                p.wait("sync", s["umm"], 8 * ng)
            for kk in range(32):
                p.dma("sync", wgu_buf[0].ap()[:, kk * cw:(kk + 1) * cw],
                      agout_g[r * E + kk * 128:r * E + (kk + 1) * 128, coff:coff + cw],
                      sem=s["gw"])
                p.dma("sync", wgu_buf[1].ap()[:, kk * cw:(kk + 1) * cw],
                      agout_u[r * E + kk * 128:r * E + (kk + 1) * 128, coff:coff + cw],
                      sem=s["uw"])
            p.wait("tensor", s["gw"])
            p.wait("scalar", s["hmul"])
            for tt in range(8):
                slot = tt % 2
                ps = psA.ap()[:, slot * 512:slot * 512 + cw]
                if tt >= 2:
                    p.wait("tensor", s["sil"], s["sil"].n - 1)
                for kk in range(32):
                    p.op("tensor", lambda t, o=ps,
                         l=xqT_sb.ap()[:, kk * TOK + tt * 128:kk * TOK + (tt + 1) * 128],
                         r=wgu_buf[0].ap()[:, kk * cw:(kk + 1) * cw],
                         st=(kk == 0), sp=(kk == 31): t.matmul(o, l, r, start=st, stop=sp),
                         sem=(s["gmm"] if kk == 31 else None))
                p.wait("scalar", s["gmm"])
                p.op("scalar", lambda sc, o=gs_sb.ap()[:, tt * 512:tt * 512 + cw], i=ps:
                     sc.activation(o, i, AF.Silu), sem=s["sil"])
            p.wait("tensor", s["uw"])
            for tt in range(8):
                slot = 2 + tt % 2
                ps = psA.ap()[:, slot * 512:slot * 512 + cw]
                if tt >= 2:
                    p.wait("tensor", s["hmul"], s["hmul"].n - 1)
                for kk in range(32):
                    p.op("tensor", lambda t, o=ps,
                         l=xqT_sb.ap()[:, kk * TOK + tt * 128:kk * TOK + (tt + 1) * 128],
                         r=wgu_buf[1].ap()[:, kk * cw:(kk + 1) * cw],
                         st=(kk == 0), sp=(kk == 31): t.matmul(o, l, r, start=st, stop=sp),
                         sem=(s["umm"] if kk == 31 else None))
                p.wait("vector", s["umm"])
                p.wait("vector", s["sil"])
                p.wait("vector", s["hq"])
                p.op("vector", lambda v, i=ps, g=gs_sb.ap()[:, tt * 512:tt * 512 + cw],
                     o=q_in.ap()[:, :cw]: v.tensor_tensor(o, g, i, ALU.mult), sem=s["hmul"])
                p.wait("vector", s["hqw"])
                emit_quant(q_in.ap()[:, :cw], cw, q_out.ap()[:, :cw], sem_out=s["hq"])
                if r == 7 and coff == 1024:
                    p.op("vector", lambda v, ww=cw: v.tensor_copy(
                        q_out.ap()[:, ww - 128:ww], q_in.ap()[:, ww - 128:ww]), sem=s["hq"])
                p.wait("sync", s["hq"])
                p.dma("sync", hq_sc[tt * 128:(tt + 1) * 128, r * FS + coff:r * FS + coff + cw],
                      q_out.ap()[:, :cw], sem=s["hqw"])
            ng += 1

        p.wait("sync", s["hqw"])
        p.wait("sync", s["cwr"])
        p.wait("tensor", s["ag2"], 1)
        p.wait("sync", s["gmm"])
        p.wait("sync", s["umm"])
        p.wait("gpsimd", s["umm"])
        p.wait("gpsimd", s["ag2"], 1)
        WDG = [16, 16, 16, 16, 16, 6]
        nfm = 0
        gcnt = 0
        dgrp_after = []
        for tg in range(2):
            if tg == 1:
                p.wait("sync", s["dgrp"], dgrp_after[-1])
            for ft in range(86):
                p.dma("sync", hqT_sb.ap()[:, ft * 512:(ft + 1) * 512],
                      hq_sc[tg * 512:(tg + 1) * 512, ft * 128:(ft + 1) * 128],
                      sem=s["hqt"], transpose=True)
            p.wait("tensor", s["hqt"])
            for dc in range(8):
                gbase = 0
                for gi, gn in enumerate(WDG):
                    par = gcnt % 2
                    if gcnt >= 2:
                        p.wait("gpsimd", s["dgrp"], gcnt - 1)
                    p.dma("gpsimd", wd_buf[par].ap()[:, :gn * 512]
                          .rearrange("p (kk d) -> p kk d", kk=gn),
                          agout_d[dc * F + gbase * 128:dc * F + (gbase + gn) * 128, :]
                          .rearrange("(kk p) d -> p kk d", p=128), sem=s["dw"])
                    p.wait("tensor", s["dw"])
                    for tt in range(4):
                        slot = tt
                        ps = psB.ap()[:, slot * 512:(slot + 1) * 512]
                        if gi == 0 and nfm + tt >= 4:
                            p.wait("tensor", s["fin"], (nfm + tt) - 3)
                        for k2 in range(gn):
                            kk = gbase + k2
                            last = (tt == 3 and k2 == gn - 1)
                            p.op("tensor", lambda t, o=ps,
                                 l=hqT_sb.ap()[:, kk * 512 + tt * 128:kk * 512 + (tt + 1) * 128],
                                 r=wd_buf[par].ap()[:, k2 * 512:(k2 + 1) * 512],
                                 st=(kk == 0), sp=False: t.matmul(o, l, r, start=st, stop=sp),
                                 sem=(s["dgrp"] if last else None))
                    gbase += gn
                    gcnt += 1
                    dgrp_after.append(s["dgrp"].n)
                for tt in range(4):
                    ps = psB.ap()[:, tt * 512:(tt + 1) * 512]
                    row0 = tg * 512 + tt * 128
                    p.op("tensor", lambda t, o=ps,
                         l=t1T_sb.ap()[:, row0:row0 + 128],
                         r=lrB_bf.ap()[:, dc * 512:(dc + 1) * 512]:
                         t.matmul(o, l, r, start=False, stop=True), sem=s["dmm"])
                    fpar = nfm % 2
                    if nfm >= 2:
                        p.wait("sync", s["fin"], nfm - 1)
                    p.dma("sync", x_fin[fpar].ap(), x_own[row0:row0 + 128, dc * 512:(dc + 1) * 512],
                          sem=s["fld"])
                    p.dma("sync", rs_fin[fpar].ap(),
                          wcout_sc[row0:row0 + 128, dc * 512:(dc + 1) * 512], sem=s["fld"])
                    p.wait("vector", s["fld"])
                    p.wait("vector", s["dmm"])
                    p.op("vector", lambda v, i=ps, x=x_fin[fpar].ap(),
                         o=o_fin[fpar].ap(): v.scalar_tensor_tensor(
                        o, i, 1.0, x, ALU.bypass, ALU.add))
                    p.op("vector", lambda v, a=o_fin[fpar].ap(), b=rs_fin[fpar].ap(),
                         o=x_fin[fpar].ap(): v.tensor_tensor(o, a, b, ALU.add), sem=s["fin"])
                    p.wait("sync", s["fin"])
                    p.dma("sync", out_own[row0:row0 + 128, dc * 512:(dc + 1) * 512],
                          x_fin[fpar].ap(), sem=s["out"])
                    nfm += 1

        p.wait("gpsimd", s["out"])

        block = es.enter_context(nc.Block())

        @block.sync
        def _(eng):
            for fn in p.sched["sync"]:
                fn(eng)

        @block.scalar
        def _(eng):
            for fn in p.sched["scalar"]:
                fn(eng)

        @block.vector
        def _(eng):
            for fn in p.sched["vector"]:
                fn(eng)

        @block.tensor
        def _(eng):
            for fn in p.sched["tensor"]:
                fn(eng)

        @block.gpsimd
        def _(eng):
            for fn in p.sched["gpsimd"]:
                fn(eng)

    _CACHE["nc"] = nc
    return nc


def kernel(L0_attn_hidden_states, L0_attn_w_qkv, L0_attn_w_c, L0_ffn_w_gate,
           L0_ffn_w_up, L0_ffn_w_down, L0_ffn_mlp_output_scale, L0_ffn_lr_A,
           L0_ffn_lr_B, L0_ffn_up_reorder_index, L0_ffn_down_reorder_index):
    x = np.asarray(L0_attn_hidden_states, dtype=np.float32)
    w_qkv = np.asarray(L0_attn_w_qkv, dtype=np.float32)
    w_c = np.asarray(L0_attn_w_c, dtype=np.float32)
    wg = np.asarray(L0_ffn_w_gate, dtype=np.float32)
    wu = np.asarray(L0_ffn_w_up, dtype=np.float32)
    wd = np.asarray(L0_ffn_w_down, dtype=np.float32)
    mscale = np.asarray(L0_ffn_mlp_output_scale, dtype=np.float32)
    lr_A = np.asarray(L0_ffn_lr_A, dtype=np.float32)
    lr_B = np.asarray(L0_ffn_lr_B, dtype=np.float32)
    uperm = np.asarray(L0_ffn_up_reorder_index, dtype=np.int64)
    dperm = np.asarray(L0_ffn_down_reorder_index, dtype=np.int64)

    x_eff = x[:, :, uperm]
    wqkv_eff = w_qkv[uperm, :]
    wcT_eff = np.ascontiguousarray(w_c[:, uperm].T)
    wg_eff = wg[dperm][:, uperm]
    wu_eff = wu[dperm][:, uperm]
    wd_eff = wd[uperm][:, dperm]
    scale_eff = mscale[uperm]
    lrA_eff = np.ascontiguousarray(lr_A[uperm, :])
    lrB_eff = np.ascontiguousarray(lr_B[:, uperm])

    slopes = _alibi_slopes(32)
    nc = build_program()

    in_maps = []
    for c in range(8):
        b, g = c // 4, c % 4
        sl = slopes[4 * c:4 * c + 4]
        rr = np.arange(128, dtype=np.float32)
        cols = np.zeros((128, NHL * 8 * 32), np.float32)
        for h in range(NHL):
            for cc in range(8):
                for t in range(32):
                    cols[:, (h * 8 + cc) * 32 + t] = sl[h] * (128.0 * t + rr - 512.0 * cc)
        bq = np.zeros((2, NHL * 512), BF)
        qq = np.arange(512, dtype=np.float64)
        for h in range(NHL):
            bfull = (-float(sl[h]) * qq / SCAL).astype(np.float32)
            hi = bfull.astype(BF)
            lo = (bfull - hi.astype(np.float32)).astype(BF)
            bq[0, h * 512:(h + 1) * 512] = hi
            bq[1, h * 512:(h + 1) * 512] = lo
        uu = np.arange(896)
        msk = np.where(uu[None, :] >= (384 + np.arange(128)[:, None]),
                       np.float32(3e38), np.float32(-3e38)).astype(np.float32)
        in_maps.append({
            "x_own": np.ascontiguousarray(x_eff[b, 1024 * g:1024 * (g + 1)]),
            "wqkv": np.ascontiguousarray(
                np.hstack([wqkv_eff[:, 512 * c:512 * (c + 1)], wqkv_eff[:, 4096:4352]])),
            "wc": wcT_eff,
            "wg_sh": np.ascontiguousarray(wg_eff[FS * c:FS * (c + 1)]),
            "wu_sh": np.ascontiguousarray(wu_eff[FS * c:FS * (c + 1)]),
            "wd_sh": np.ascontiguousarray(wd_eff[DS * c:DS * (c + 1)]),
            "scale_sh": np.ascontiguousarray(scale_eff[DS * c:DS * (c + 1)].reshape(4, 128).T),
            "lrA": lrA_eff,
            "lrB": lrB_eff,
            "aux_bias": cols,
            "aux_bq": bq,
            "aux_mask": msk,
            "aux_ones": np.ones((128, 128), BF),
        })

    res = run_bass_kernel_spmd(nc, in_maps, core_ids=list(range(8)))

    out_eff = np.empty((2, S, E), np.float32)
    for c in range(8):
        b, g = c // 4, c % 4
        out_eff[b, 1024 * g:1024 * (g + 1)] = res.results[c]["out_own"]
    out = np.empty_like(out_eff)
    out[:, :, uperm] = out_eff
    return out


# version 11
# speedup vs baseline: 1.0862x; 1.0363x over previous
import math
import numpy as np
import ml_dtypes
from contextlib import ExitStack

import concourse.bass as bass
import concourse.mybir as mybir
from concourse.bass_utils import run_bass_kernel_spmd
from concourse.aot_env import allow_ici_all_to_all

F32 = mybir.dt.float32
BF16 = mybir.dt.bfloat16
AF = mybir.ActivationFunctionType
ALU = mybir.AluOpType
AX = mybir.AxisListType
BF = ml_dtypes.bfloat16

S, E, F, R = 4096, 4096, 11008, 64
TOK = 1024
NHL = 4
QKV = NHL * 128 + 256
FS, FSP, DS = F // 8, 1408, E // 8
SCAL = float(128) ** -0.5
M_H = float(np.float32(1.5 * 2 ** 22))
M_I = float(np.float32(1.5 * 2 ** 23))
EPS = 1e-5
QW = 512

_CACHE = {}


def _alibi_slopes(n=32):
    cp2 = 2 ** int(math.floor(math.log2(n)))
    base = 2.0 ** (-2.0 ** (-(math.log2(cp2) - 3)))
    slopes = base ** np.arange(1, cp2 + 1, dtype=np.float64)
    if cp2 != n:
        eb = 2.0 ** (-2.0 ** (-(math.log2(2 * cp2) - 3)))
        rem = min(cp2, n - cp2)
        slopes = np.concatenate([slopes, eb ** np.arange(1, 1 + 2 * rem, 2, dtype=np.float64)])
    return slopes.astype(np.float32)


class Sem:
    def __init__(self, es, nc, name):
        self.h = es.enter_context(nc.semaphore(name))
        self.n = 0


class Prog:
    def __init__(self):
        self.sched = {k: [] for k in ("sync", "scalar", "vector", "tensor", "gpsimd")}

    def wait(self, eng, sem, val=None):
        v = sem.n if val is None else val
        if v > 0:
            self.sched[eng].append(lambda e, s=sem.h, vv=v: e.wait_ge(s, vv))

    def op(self, eng, fn, sem=None, k=1):
        if sem is None:
            self.sched[eng].append(fn)
        else:
            sem.n += k
            self.sched[eng].append(lambda e, f=fn, s=sem.h, kk=k: f(e).then_inc(s, kk))

    def dma(self, eng, out, in_, sem=None, transpose=False):
        self.op(eng, lambda e, o=out, i=in_, t=transpose: e.dma_start(out=o, in_=i, transpose=t),
                sem=sem, k=16)


def build_program():
    if "nc" in _CACHE:
        return _CACHE["nc"]
    nc = bass.Bass("TRN2", target_bir_lowering=False)

    x_own = nc.dram_tensor("x_own", [TOK, E], F32, kind="ExternalInput").ap()
    wqkv = nc.dram_tensor("wqkv", [E, QKV], F32, kind="ExternalInput").ap()
    wc = nc.dram_tensor("wc", [E, E], F32, kind="ExternalInput").ap()
    wg_sh = nc.dram_tensor("wg_sh", [FS, E], F32, kind="ExternalInput").ap()
    wu_sh = nc.dram_tensor("wu_sh", [FS, E], F32, kind="ExternalInput").ap()
    wd_sh = nc.dram_tensor("wd_sh", [DS, F], F32, kind="ExternalInput").ap()
    scale_sh = nc.dram_tensor("scale_sh", [128, 4], F32, kind="ExternalInput").ap()
    lrA = nc.dram_tensor("lrA", [E, R], F32, kind="ExternalInput").ap()
    lrB = nc.dram_tensor("lrB", [R, E], F32, kind="ExternalInput").ap()
    aux_bias = nc.dram_tensor("aux_bias", [128, NHL * 8 * 32], F32, kind="ExternalInput").ap()
    aux_bq = nc.dram_tensor("aux_bq", [2, NHL * 512], BF16, kind="ExternalInput").ap()
    aux_mask = nc.dram_tensor("aux_mask", [128, 896], F32, kind="ExternalInput").ap()
    aux_ones = nc.dram_tensor("aux_ones", [128, 128], BF16, kind="ExternalInput").ap()
    out_own = nc.dram_tensor("out_own", [TOK, E], F32, kind="ExternalOutput").ap()

    xnown_sc = nc.dram_tensor("xnown_sc", [TOK, E], BF16).ap()
    agout_x = nc.dram_tensor("agout_x", [8 * TOK, E], BF16).ap()
    xq_sc = nc.dram_tensor("xq_sc", [TOK, E], BF16).ap()
    qkT_sc = nc.dram_tensor("qkT_sc", [2 * QKV, S], BF16).ap()
    hq_sc = nc.dram_tensor("hq_sc", [TOK, F], BF16).ap()
    wcout_sc = nc.dram_tensor("wcout_sc", [TOK, E], BF16).ap()
    wgq_sc = nc.dram_tensor("wgq_sc", [FSP, E], BF16).ap()
    wuq_sc = nc.dram_tensor("wuq_sc", [FSP, E], BF16).ap()
    wdq_sc = nc.dram_tensor("wdq_sc", [DS, F], BF16).ap()
    agin_g = nc.dram_tensor("agin_g", [E, FSP], BF16).ap()
    agin_u = nc.dram_tensor("agin_u", [E, FSP], BF16).ap()
    agin_d = nc.dram_tensor("agin_d", [F, DS], BF16).ap()
    agout_g = nc.dram_tensor("agout_g", [8 * E, FSP], BF16).ap()
    agout_u = nc.dram_tensor("agout_u", [8 * E, FSP], BF16).ap()
    agout_d = nc.dram_tensor("agout_d", [8 * F, DS], BF16).ap()
    a2a_in = nc.dram_tensor("a2a_in", [8 * 512, TOK], BF16).ap()
    a2a_out = nc.dram_tensor("a2a_out", [8 * 512, TOK], BF16).ap()
    wc_bf = nc.dram_tensor("wc_bf", [E, E], BF16).ap()

    G8 = [list(range(8))]
    p = Prog()
    SBUF_LIM = 212863

    with ExitStack() as es:
        ARENA = 151552
        arena_base = [None]

        def alloc(name, shape, dtype, at=None, tdest=False):
            if at is None:
                return nc.alloc_sbuf_tensor(name, shape, dtype)
            itemsz = 2 if dtype == BF16 else 4
            per_part = int(np.prod(shape[1:])) * itemsz
            o = arena_base[0] + (at - (1 << 20))
            assert (at - (1 << 20)) + per_part <= ARENA, (name, at - (1 << 20) + per_part)
            return nc.alloc_sbuf_tensor_at(name, shape, dtype, offset=o)

        q_a = alloc("q_a", [128, QW], F32)
        q_c2 = alloc("q_c2", [128, QW], F32)
        q_c4 = alloc("q_c4", [128, QW], F32)
        q_t1 = alloc("q_t1", [128, QW], F32)
        q_in = alloc("q_in", [128, QW], F32)
        q_r1 = alloc("q_r1", [128, QW], BF16)
        q_r2 = alloc("q_r2", [128, QW], BF16)
        q_r3 = alloc("q_r3", [128, QW], BF16)
        q_qa = alloc("q_qa", [128, QW], BF16)
        q_out = alloc("q_out", [128, QW], BF16)
        q_bm = alloc("q_bm", [128, QW // 16], F32)
        q_r6 = alloc("q_r6", [128, QW // 16], F32)
        q_bm6 = alloc("q_bm6", [128, QW // 16], F32)
        bias_sb = alloc("bias_sb", [128, NHL * 8 * 32], F32)
        bq_sb = alloc("bq_sb", [2, NHL * 512], BF16)
        mask_sb = alloc("mask_sb", [128, 896], F32)
        ones_sb = alloc("ones_sb", [128, 128], BF16)
        t1T_sb = alloc("t1T_sb", [R, TOK], BF16)
        stat_a = alloc("stat_a", [128, 8], F32)
        stat_b = alloc("stat_b", [128, 8], F32)
        st_mean = alloc("st_mean", [128, 1], F32)
        st_var = alloc("st_var", [128, 1], F32)
        st_rstd = alloc("st_rstd", [128, 1], F32)
        st_nb = alloc("st_nb", [128, 1], F32)
        ev_bf = [alloc(f"ev_bf{i}", [128, 512], BF16) for i in range(2)]
        scale_sb = alloc("scale_sb", [128, 4], F32)
        rec_sb = alloc("rec_sb", [128, 512], F32)
        lnr_sb = alloc("lnr_sb", [128, 512], F32)
        attn_bf = alloc("attn_bf", [128, 512], BF16)
        lrA_bf = alloc("lrA_bf", [128, 32 * R], BF16)
        lrB_bf = alloc("lrB_bf", [R, E], BF16)
        x_fin = [alloc(f"x_fin{i}", [128, 512], F32) for i in range(2)]
        rs_fin = [alloc(f"rs_fin{i}", [128, 512], BF16) for i in range(2)]
        o_fin = [alloc(f"o_fin{i}", [128, 512], F32) for i in range(2)]
        _arena_t = nc.alloc_sbuf_tensor("overlay_arena", [128, ARENA], mybir.dt.uint8)
        arena_base[0] = nc.lookup_mloc(_arena_t).addr
        OV = 1 << 20

        xb = [alloc(f"xb{i}", [128, E], F32, at=OV + i * 16384) for i in range(2)]
        wqkv_bf = alloc("wqkv_bf", [128, 32 * QKV], BF16, at=OV + 32768)
        rhs_buf = [alloc(f"rhs{i}", [128, 32 * 512], BF16,
                         at=(OV + 0 if i == 0 else OV + 81920)) for i in range(2)]
        kT2 = alloc("kT2", [128, 2 * S], BF16, at=OV)
        v2 = alloc("v2", [128, 2 * S], BF16, at=OV + 16384)
        qT_buf = [alloc(f"qT{i}", [128, 512], BF16, at=OV + 32768 + i * 1024) for i in range(2)]
        probs_sb = alloc("probs_sb", [128, 8 * 512], BF16, at=OV + 34816)
        wq_in = [alloc(f"wq_in{i}", [128, QW], F32, at=OV + 131072 + i * 2048) for i in range(2)]
        wq_tr = [alloc(f"wq_tr{i}", [128, FSP], BF16, at=OV + 135168 + i * 2816) for i in range(2)]
        a2a_res = alloc("a2a_res", [128, 8 * 32 * 128], BF16, at=OV)
        wc_rhs = [alloc(f"wc_rhs{i}", [128, 32 * 512], BF16, at=OV + 65536 + i * 32768) for i in range(2)]
        xqT_sb = alloc("xqT_sb", [128, 32 * TOK], BF16, at=OV)
        wgu_buf = [alloc(f"wgu{i}", [128, 32 * 512], BF16, at=OV + 65536 + i * 32768) for i in range(2)]
        gs_sb = alloc("gs_sb", [128, 8 * 512], F32, at=OV + 131072)
        hqT_sb = alloc("hqT_sb", [128, 86 * 512], BF16, at=OV)
        wd_buf = [alloc(f"wd_buf{i}", [128, 16 * 512], BF16, at=OV + 90112 + i * 16384) for i in range(2)]

        psA = es.enter_context(nc.psum_tensor("psA", [128, 2048], F32))
        psB = es.enter_context(nc.psum_tensor("psB", [128, 2048], F32))

        sn = ("aux ldx stat app xnw xqv xqw agx wqkvl rhs mm ev qkw lora lrb scl "
              "kv qt st exp msk pv rec evac a2w a2a "
              "wqi wqv wqw wqt agw ag0 ag1 ag2 "
              "cw ca cmm cev cwr "
              "xqt gw uw gmm umm sil hmul hq hqw "
              "hqt dw dgrp dmm fin fld out vsq ssq cwp").split()
        s = {n: Sem(es, nc, "s_" + n) for n in sn}

        def emit_quant(src_ap, w, dst_ap, pp_scale=None, sem_out=None):
            nb = w // 16
            src3 = src_ap.rearrange("p (b k) -> p b k", k=16)
            p.op("vector", lambda v, ss=src3, n=nb: v.tensor_reduce(
                q_bm.ap()[:, :n], ss, AX.X, ALU.max, apply_absolute_value=True),
                sem=s["vsq"])
            p.wait("vector", s["vsq"])
            p.op("vector", lambda v, n=nb: v.tensor_scalar(
                q_bm.ap()[:, :n], q_bm.ap()[:, :n], 1e-30, None, ALU.max),
                sem=s["vsq"])
            p.wait("vector", s["vsq"])
            p.op("vector", lambda v, n=nb: v.reciprocal(q_r6.ap()[:, :n], q_bm.ap()[:, :n]),
                 sem=s["vsq"])
            if pp_scale is None:
                p.op("vector", lambda v, n=nb: v.tensor_scalar(
                    q_bm6.ap()[:, :n], q_bm.ap()[:, :n], 1.0 / 6.0, None, ALU.mult))
            else:
                p.op("vector", lambda v, n=nb, pa=pp_scale: v.tensor_scalar(
                    q_bm6.ap()[:, :n], q_bm.ap()[:, :n], 1.0 / 6.0, pa, ALU.mult, ALU.mult))
            p.wait("vector", s["vsq"])
            r6b = q_r6.ap()[:, :nb].unsqueeze(2).broadcast_to([128, nb, 16])
            bm6b = q_bm6.ap()[:, :nb].unsqueeze(2).broadcast_to([128, nb, 16])
            a3 = q_a.ap()[:, :w].rearrange("p (b k) -> p b k", k=16)
            p.op("vector", lambda v, ss=src3, r=r6b, a=a3: v.scalar_tensor_tensor(
                a, ss, 6.0, r, ALU.mult, ALU.mult))
            p.op("vector", lambda v, w=w: v.tensor_scalar(
                q_c2.ap()[:, :w], q_a.ap()[:, :w], -2.0, 2.0, ALU.max, ALU.min))
            p.op("vector", lambda v, w=w: v.tensor_scalar(
                q_c4.ap()[:, :w], q_a.ap()[:, :w], -4.0, 4.0, ALU.max, ALU.min))
            p.op("vector", lambda v, w=w: v.tensor_scalar(
                q_r1.ap()[:, :w], q_c2.ap()[:, :w], M_H, -M_H, ALU.add, ALU.add))
            p.op("vector", lambda v, w=w: v.tensor_sub(
                q_t1.ap()[:, :w], q_c4.ap()[:, :w], q_c2.ap()[:, :w]))
            p.op("vector", lambda v, w=w: v.tensor_scalar(
                q_r2.ap()[:, :w], q_t1.ap()[:, :w], M_I, -M_I, ALU.add, ALU.add))
            p.op("vector", lambda v, w=w: v.tensor_sub(
                q_c2.ap()[:, :w], q_a.ap()[:, :w], q_c4.ap()[:, :w]))
            p.op("vector", lambda v, w=w: v.tensor_scalar(
                q_t1.ap()[:, :w], q_c2.ap()[:, :w], 0.5, M_I, ALU.mult, ALU.add))
            p.op("vector", lambda v, w=w: v.tensor_scalar(
                q_r3.ap()[:, :w], q_t1.ap()[:, :w], -M_I, 2.0, ALU.add, ALU.mult))
            p.op("vector", lambda v, w=w: v.tensor_add(
                q_qa.ap()[:, :w], q_r1.ap()[:, :w], q_r2.ap()[:, :w]))
            p.op("vector", lambda v, w=w: v.tensor_add(
                q_r1.ap()[:, :w], q_qa.ap()[:, :w], q_r3.ap()[:, :w]))
            d3 = dst_ap.rearrange("p (b k) -> p b k", k=16)
            r13 = q_r1.ap()[:, :w].rearrange("p (b k) -> p b k", k=16)
            p.op("vector", lambda v, d=d3, r=r13, m=bm6b: v.tensor_tensor(d, r, m, ALU.mult),
                 sem=sem_out)

        p.dma("sync", bias_sb.ap(), aux_bias, sem=s["aux"])
        p.dma("sync", bq_sb.ap(), aux_bq, sem=s["aux"])
        p.dma("sync", mask_sb.ap(), aux_mask, sem=s["aux"])
        p.dma("sync", ones_sb.ap(), aux_ones, sem=s["aux"])
        p.dma("sync", scale_sb.ap(), scale_sh, sem=s["scl"])
        p.dma("gpsimd", wqkv_bf.ap().rearrange("p (kk n) -> p kk n", kk=32),
              wqkv.rearrange("(kk p) n -> p kk n", p=128), sem=s["wqkvl"])
        p.dma("gpsimd", lrA_bf.ap().rearrange("p (kk r) -> p kk r", kk=32),
              lrA.rearrange("(kk p) r -> p kk r", p=128), sem=s["lrb"])
        p.dma("gpsimd", lrB_bf.ap(), lrB, sem=s["lrb"])
        for i in range(8):
            p.dma("gpsimd", wc_bf[i * 512:(i + 1) * 512, :],
                  wc[i * 512:(i + 1) * 512, :], sem=s["cwp"])

        NCK = E // QW
        for i in range(8):
            par = i % 2
            if i >= 2:
                p.wait("sync", s["app"], 2 * NCK * (i - 1))
            p.dma("sync", xb[par].ap(), x_own[i * 128:(i + 1) * 128, :], sem=s["ldx"])
            p.wait("vector", s["ldx"], (i + 1) * 16)
            for k in range(NCK):
                ck = xb[par].ap()[:, k * QW:(k + 1) * QW]
                last = (k == NCK - 1)
                p.op("vector", lambda v, c=ck, kk=k: v.scalar_tensor_tensor(
                    q_t1.ap()[:, :QW], c, 1.0, c, ALU.bypass, ALU.mult,
                    accum_out=stat_b.ap()[:, kk:kk + 1]),
                    sem=(s["vsq"] if last else None))
                p.op("vector", lambda v, c=ck, kk=k: v.tensor_reduce(
                    stat_a.ap()[:, kk:kk + 1], c, AX.X, ALU.add),
                    sem=(s["vsq"] if last else None))
            p.wait("vector", s["vsq"])
            p.op("vector", lambda v: v.tensor_reduce(st_mean.ap(), stat_a.ap()[:, 0:NCK], AX.X, ALU.add), sem=s["vsq"])
            p.op("vector", lambda v: v.tensor_reduce(st_var.ap(), stat_b.ap()[:, 0:NCK], AX.X, ALU.add), sem=s["vsq"])
            p.wait("vector", s["vsq"])
            p.op("vector", lambda v: v.tensor_scalar(st_mean.ap(), st_mean.ap(), 1.0 / E, None, ALU.mult), sem=s["vsq"])
            p.op("vector", lambda v: v.tensor_scalar(st_var.ap(), st_var.ap(), 1.0 / E, None, ALU.mult), sem=s["vsq"])
            p.wait("vector", s["vsq"])
            p.op("vector", lambda v: v.tensor_scalar(
                q_t1.ap()[:, 0:1], st_mean.ap(), st_mean.ap()[:, 0:1], None, ALU.mult), sem=s["vsq"])
            p.wait("vector", s["vsq"])
            p.op("vector", lambda v: v.tensor_sub(
                st_var.ap(), st_var.ap(), q_t1.ap()[:, 0:1]), sem=s["vsq"])
            p.wait("vector", s["vsq"])
            p.op("vector", lambda v: v.tensor_scalar(
                st_var.ap(), st_var.ap(), EPS, None, ALU.add), sem=s["stat"])
            p.wait("scalar", s["stat"])
            p.op("scalar", lambda sc: sc.activation(st_rstd.ap(), st_var.ap(), AF.Ln), sem=s["ssq"])
            p.wait("scalar", s["ssq"])
            p.op("scalar", lambda sc: sc.activation(st_rstd.ap(), st_rstd.ap(), AF.Exp, scale=-0.5),
                 sem=s["stat"])
            p.wait("vector", s["stat"])
            p.op("vector", lambda v: v.tensor_scalar(
                st_nb.ap(), st_mean.ap(), st_rstd.ap()[:, 0:1], -1.0, ALU.mult, ALU.mult),
                sem=s["stat"])
            p.wait("vector", s["stat"])
            p.wait("scalar", s["stat"])
            nbb = st_nb.ap()[:, 0:1].broadcast_to([128, QW])
            for k in range(NCK):
                ck = xb[par].ap()[:, k * QW:(k + 1) * QW]
                p.wait("vector", s["xqv"])
                p.op("vector", lambda v, c=ck, nb=nbb: v.scalar_tensor_tensor(
                    q_in.ap()[:, :QW], c, st_rstd.ap()[:, 0:1], nb, ALU.mult, ALU.add),
                    sem=s["app"])
                p.wait("vector", s["xnw"])
                p.op("vector", lambda v, c=ck, nb=nbb: v.scalar_tensor_tensor(
                    q_qa.ap()[:, :QW], c, st_rstd.ap()[:, 0:1], nb, ALU.mult, ALU.add),
                    sem=s["app"])
                p.wait("scalar", s["app"])
                p.dma("scalar", xnown_sc[i * 128:(i + 1) * 128, k * QW:(k + 1) * QW],
                      q_qa.ap()[:, :QW], sem=s["xnw"])
                p.wait("vector", s["xqw"])
                p.wait("vector", s["xnw"])
                emit_quant(q_in.ap()[:, :QW], QW, q_out.ap()[:, :QW], sem_out=s["xqv"])
                if k == NCK - 1:
                    p.op("vector", lambda v: v.tensor_copy(
                        q_out.ap()[:, QW - 128:QW], q_in.ap()[:, QW - 128:QW]), sem=s["xqv"])
                p.wait("sync", s["xqv"])
                p.dma("sync", xq_sc[i * 128:(i + 1) * 128, k * QW:(k + 1) * QW],
                      q_out.ap()[:, :QW], sem=s["xqw"])
        p.wait("gpsimd", s["xnw"])
        p.op("gpsimd", lambda g: g.collective_compute(
            "AllGather", ALU.bypass, replica_groups=G8, ins=[xnown_sc], outs=[agout_x]),
            sem=s["agx"])

        wq_units = []

        def mk_unit(srcd, dstd, r0, rh, srh, colk, w, pp, pt):
            def load(par, q):
                p.dma(q, wq_in[par].ap()[0:rh, :w], srcd[r0:r0 + rh, colk:colk + w],
                      sem=s["wqi"])

            def quant(par, q):
                p.wait("vector", s["wqi"])
                p.wait("vector", s["xqw"])
                p.wait("vector", s["wqw"])
                if pp is not None:
                    p.wait("vector", s["scl"], 16)
                emit_quant(wq_in[par].ap()[0:128, :w], w, q_out.ap()[0:128, :w],
                           pp_scale=pp, sem_out=s["wqv"])
                if pt:
                    if pp is None:
                        p.op("vector", lambda v, ww=w: v.tensor_copy(
                            q_out.ap()[:, ww - 128:ww], wq_in[par].ap()[:, ww - 128:ww]),
                            sem=s["wqv"])
                    else:
                        p.op("vector", lambda v, ww=w, pa=pp: v.tensor_scalar(
                            q_out.ap()[:, ww - 128:ww], wq_in[par].ap()[:, ww - 128:ww],
                            pa, None, ALU.mult), sem=s["wqv"])
                p.wait(q, s["wqv"])
                p.dma(q, dstd[r0:r0 + srh, colk:colk + w], q_out.ap()[0:srh, :w],
                      sem=s["wqw"])
            return load, quant

        for (srcd, dstd, nr, pp_on, fax) in ((wg_sh, wgq_sc, FS, False, False),
                                             (wu_sh, wuq_sc, FS, False, False),
                                             (wd_sh, wdq_sc, DS, True, True)):
            ncols = F if fax else E
            r0 = 0
            while r0 < nr:
                rh = min(128, nr - r0)
                srh = 128 if (nr == FS and r0 + 128 == FSP) else rh
                pp = scale_sb.ap()[:, r0 // 128:r0 // 128 + 1] if pp_on else None
                for colk in range(0, ncols, QW):
                    w = min(QW, ncols - colk)
                    pt = (colk + w == ncols)
                    wq_units.append(mk_unit(srcd, dstd, r0, rh, srh, colk, w, pp, pt))
                r0 += 128
        wq_state = {"i": 0, "loaded": 0}
        wqv_after = []

        def drain_wq(n, q="scalar"):
            for _ in range(n):
                i = wq_state["i"]
                if i >= len(wq_units):
                    return
                while wq_state["loaded"] < min(i + 2, len(wq_units)):
                    j = wq_state["loaded"]
                    if j >= 2:
                        p.wait(q, s["wqv"], wqv_after[j - 2])
                    wq_units[j][0](j % 2, q)
                    wq_state["loaded"] += 1
                wq_units[i][1](i % 2, q)
                wqv_after.append(s["wqv"].n)
                wq_state["i"] += 1

        p.wait("tensor", s["wqkvl"])
        p.wait("sync", s["agx"], 1)
        p.wait("sync", s["app"])
        nmm = 0
        mm_after_chunk = []
        for ch in range(16):
            bb, tc = ch // 8, ch % 8
            par = ch % 2
            if ch >= 2:
                p.wait("sync", s["mm"], mm_after_chunk[ch - 2])
            for kk in range(32):
                p.dma("sync", rhs_buf[par].ap()[:, kk * 512:(kk + 1) * 512],
                      agout_x[bb * S + tc * 512:bb * S + (tc + 1) * 512, kk * 128:(kk + 1) * 128],
                      sem=s["rhs"], transpose=True)
            p.wait("tensor", s["rhs"])
            for m in range(6):
                slot = nmm % 4
                ps = psA.ap()[:, slot * 512:(slot + 1) * 512]
                if nmm >= 4:
                    p.wait("tensor", s["ev"], nmm - 3)
                for kk in range(32):
                    p.op("tensor", lambda t, o=ps,
                         l=wqkv_bf.ap()[:, kk * QKV + m * 128:kk * QKV + (m + 1) * 128],
                         r=rhs_buf[par].ap()[:, kk * 512:(kk + 1) * 512],
                         st=(kk == 0), sp=(kk == 31): t.matmul(o, l, r, start=st, stop=sp),
                         sem=(s["mm"] if kk == 31 else None))
                p.wait("vector", s["mm"])
                if nmm >= 2:
                    p.wait("vector", s["qkw"], (nmm - 1) * 16)
                evp = ev_bf[nmm % 2].ap()
                p.op("vector", lambda v, o=evp, i=ps: v.tensor_copy(o, i), sem=s["ev"])
                p.wait("scalar", s["ev"])
                p.dma("scalar", qkT_sc[bb * QKV + m * 128:bb * QKV + (m + 1) * 128,
                                       tc * 512:(tc + 1) * 512], evp, sem=s["qkw"])
                nmm += 1
            mm_after_chunk.append(s["mm"].n)
            drain_wq(4)
        p.wait("tensor", s["lrb"], 32)
        for tc in range(2):
            par = tc % 2
            p.wait("sync", s["mm"], mm_after_chunk[14 + tc])
            for kk in range(32):
                p.dma("sync", rhs_buf[par].ap()[:, kk * 512:(kk + 1) * 512],
                      xnown_sc[tc * 512:(tc + 1) * 512, kk * 128:(kk + 1) * 128],
                      sem=s["rhs"], transpose=True)
            p.wait("tensor", s["rhs"])
            ps = psB.ap()[0:R, tc * 512:(tc + 1) * 512]
            for kk in range(32):
                p.op("tensor", lambda t, o=ps, l=lrA_bf.ap()[:, kk * R:(kk + 1) * R],
                     r=rhs_buf[par].ap()[:, kk * 512:(kk + 1) * 512],
                     st=(kk == 0), sp=(kk == 31): t.matmul(o, l, r, start=st, stop=sp),
                     sem=(s["mm"] if kk == 31 else None))
            p.wait("vector", s["mm"])
            p.op("vector", lambda v, o=t1T_sb.ap()[:, tc * 512:(tc + 1) * 512], i=ps:
                 v.tensor_copy(o, i), sem=s["lora"])

        p.wait("sync", s["qkw"])
        p.wait("sync", s["lora"])
        for bb in range(2):
            p.dma("sync", kT2.ap()[:, bb * S:(bb + 1) * S],
                  qkT_sc[bb * QKV + 512:bb * QKV + 640, :], sem=s["kv"])
            for t in range(32):
                p.dma("sync", v2.ap()[:, bb * S + t * 128:bb * S + (t + 1) * 128],
                      qkT_sc[bb * QKV + 640:bb * QKV + 768, t * 128:(t + 1) * 128],
                      sem=s["kv"], transpose=True)
        p.wait("tensor", s["kv"])
        p.wait("tensor", s["ev"])
        p.wait("tensor", s["lora"])
        for eng in ("tensor", "scalar", "vector"):
            p.wait(eng, s["aux"])

        it = 0
        st_after_it = []
        for bb in range(2):
            for h in range(NHL):
                for c in range(8):
                    nk = 4 * (c + 1)
                    par = it % 2
                    apar = it % 2
                    if it >= 2:
                        p.wait("sync", s["st"], st_after_it[it - 2])
                    p.dma("sync", qT_buf[par].ap(),
                          qkT_sc[bb * QKV + h * 128:bb * QKV + (h + 1) * 128,
                                 c * 512:(c + 1) * 512], sem=s["qt"])
                    p.wait("tensor", s["qt"])
                    attn_ps = psB.ap()[:, apar * 512:(apar + 1) * 512]
                    rows_ps = psB.ap()[:, (2 + apar) * 512:(3 + apar) * 512]
                    if it >= 2:
                        p.wait("tensor", s["evac"], it - 1)
                    bias_base = (h * 8 + c) * 32
                    ngrp = (nk + 3) // 4
                    for grp in range(ngrp):
                        tlist = list(range(grp * 4, min(grp * 4 + 4, nk)))
                        if grp >= 1:
                            p.wait("tensor", s["exp"])
                        if grp >= 2 or (grp == 0 and it >= 1):
                            p.wait("scalar", s["pv"])
                        for t in tlist:
                            slot = t % 4
                            stp = psA.ap()[:, slot * 512:(slot + 1) * 512]
                            p.op("tensor", lambda tt, o=stp,
                                 l=kT2.ap()[:, bb * S + t * 128:bb * S + (t + 1) * 128],
                                 r=qT_buf[par].ap(): tt.matmul(o, l, r, start=True, stop=False))
                            p.op("tensor", lambda tt, o=stp, l=ones_sb.ap()[0:2, :],
                                 r=bq_sb.ap()[0:2, h * 512:(h + 1) * 512]:
                                 tt.matmul(o, l, r, start=False, stop=True), sem=s["st"])
                        msk_base = s["msk"].n
                        for t in tlist:
                            if t < nk - 4:
                                continue
                            slot = t % 4
                            stp = psA.ap()[:, slot * 512:(slot + 1) * 512]
                            d = t - (nk - 4)
                            p.wait("vector", s["st"], s["st"].n - (tlist[-1] - t))
                            p.op("vector", lambda v, o=stp,
                                 m=mask_sb.ap()[:, 384 - 128 * d:896 - 128 * d]:
                                 v.tensor_tensor(o, o, m, ALU.min), sem=s["msk"])
                        for t in tlist:
                            slot = t % 4
                            stp = psA.ap()[:, slot * 512:(slot + 1) * 512]
                            prb = probs_sb.ap()[:, (t % 8) * 512:(t % 8) * 512 + 512]
                            p.wait("scalar", s["st"], s["st"].n - (tlist[-1] - t))
                            if t >= nk - 4:
                                p.wait("scalar", s["msk"], msk_base + (t - (nk - 4) + 1))
                            p.op("scalar", lambda sc, o=prb, i=stp,
                                 ba=bias_sb.ap()[:, bias_base + t:bias_base + t + 1]:
                                 sc.activation(o, i, AF.Exp, bias=ba, scale=SCAL), sem=s["exp"])
                        p.wait("tensor", s["exp"])
                        for t in tlist:
                            prb = probs_sb.ap()[:, (t % 8) * 512:(t % 8) * 512 + 512]
                            p.op("tensor", lambda tt, o=attn_ps,
                                 l=v2.ap()[:, bb * S + t * 128:bb * S + (t + 1) * 128],
                                 r=prb, st=(t == 0), sp=(t == nk - 1):
                                 tt.matmul(o, l, r, start=st, stop=sp), sem=s["pv"])
                            p.op("tensor", lambda tt, o=rows_ps, l=ones_sb.ap(), r=prb,
                                 st=(t == 0), sp=(t == nk - 1):
                                 tt.matmul(o, l, r, start=st, stop=sp), sem=s["pv"])
                    st_after_it.append(s["st"].n)
                    p.wait("scalar", s["pv"])
                    if it >= 1:
                        p.wait("scalar", s["evac"], it)
                    p.op("scalar", lambda sc, i=rows_ps: sc.activation(lnr_sb.ap(), i, AF.Ln))
                    p.op("scalar", lambda sc: sc.activation(rec_sb.ap(), lnr_sb.ap(), AF.Exp,
                                                            scale=-1.0), sem=s["rec"])
                    p.wait("vector", s["rec"])
                    if it >= 1:
                        p.wait("vector", s["a2w"], it * 16)
                    p.op("vector", lambda v, i=attn_ps: v.tensor_tensor(
                        attn_bf.ap(), i, rec_sb.ap(), ALU.mult), sem=s["evac"])
                    p.wait("scalar", s["evac"])
                    j = bb * 4 + c // 2
                    p.dma("scalar", a2a_in[j * 512 + h * 128:j * 512 + (h + 1) * 128,
                                           (c % 2) * 512:(c % 2) * 512 + 512],
                          attn_bf.ap(), sem=s["a2w"])
                    it += 1
                    drain_wq(2)

        p.wait("gpsimd", s["a2w"])

        def _a2a_op(g):
            with allow_ici_all_to_all():
                return g.collective_compute(
                    "AllToAll", ALU.bypass, replica_groups=G8, ins=[a2a_in], outs=[a2a_out])
        p.op("gpsimd", _a2a_op, sem=s["a2a"])

        p.wait("sync", s["wqw"], 176 * 16)
        jj = 0
        for (src_, agi, njt, wrow) in ((wgq_sc, agin_g, 32, FSP), (wuq_sc, agin_u, 32, FSP)):
            for j in range(njt):
                par = jj % 2
                if jj >= 2:
                    p.wait("sync", s["agw"], (jj - 1) * 16)
                p.dma("sync", wq_tr[par].ap()[:, :wrow], src_[:, j * 128:(j + 1) * 128],
                      sem=s["wqt"], transpose=True)
                p.wait("scalar", s["wqt"])
                p.dma("scalar", agi[j * 128:(j + 1) * 128, :], wq_tr[par].ap()[:, :wrow],
                      sem=s["agw"])
                jj += 1
        p.wait("gpsimd", s["agw"], 64 * 16)
        p.op("gpsimd", lambda g: g.collective_compute(
            "AllGather", ALU.bypass, replica_groups=G8,
            ins=[agin_g], outs=[agout_g]), sem=s["ag0"])
        p.op("gpsimd", lambda g: g.collective_compute(
            "AllGather", ALU.bypass, replica_groups=G8,
            ins=[agin_u], outs=[agout_u]), sem=s["ag1"])

        p.wait("sync", s["a2a"], 1)
        p.wait("sync", s["cwp"])

        def load_a2a_tt(tt):
            for kk in range(32):
                p.dma("sync", a2a_res.ap()[:, (tt * 32 + kk) * 128:(tt * 32 + kk + 1) * 128],
                      a2a_out[kk * 128:(kk + 1) * 128, tt * 128:(tt + 1) * 128],
                      sem=s["ca"])

        def load_wc_dc(dc, par):
            for kk in range(32):
                p.dma("sync", wc_rhs[par].ap()[:, kk * 512:(kk + 1) * 512],
                      wc_bf[dc * 512:(dc + 1) * 512, kk * 128:(kk + 1) * 128],
                      sem=s["cw"], transpose=True)

        load_a2a_tt(0)
        load_wc_dc(0, 0)
        load_wc_dc(1, 1)
        for tt in range(1, 8):
            load_a2a_tt(tt)
        ncm = 0
        for dc in range(8):
            par = dc % 2
            p.wait("tensor", s["cw"], (dc + 1) * 32 * 16)
            for tt in range(8):
                p.wait("tensor", s["ca"], (tt + 1) * 32 * 16)
                slot = ncm % 4
                ps = psB.ap()[:, slot * 512:(slot + 1) * 512]
                if ncm >= 4:
                    p.wait("tensor", s["cev"], ncm - 3)
                for kk in range(32):
                    p.op("tensor", lambda t, o=ps,
                         l=a2a_res.ap()[:, (tt * 32 + kk) * 128:(tt * 32 + kk + 1) * 128],
                         r=wc_rhs[par].ap()[:, kk * 512:(kk + 1) * 512],
                         st=(kk == 0), sp=(kk == 31): t.matmul(o, l, r, start=st, stop=sp),
                         sem=(s["cmm"] if kk == 31 else None))
                p.wait("scalar", s["cmm"])
                if ncm >= 2:
                    p.wait("scalar", s["cwr"], (ncm - 1) * 16)
                evp = ev_bf[ncm % 2].ap()
                p.op("scalar", lambda sc, o=evp, i=ps: sc.activation(o, i, AF.Copy), sem=s["cev"])
                p.wait("scalar", s["cev"])
                p.dma("scalar", wcout_sc[tt * 128:(tt + 1) * 128, dc * 512:(dc + 1) * 512],
                      evp, sem=s["cwr"])
                ncm += 1
            p.wait("sync", s["cwr"])
            p.wait("tensor", s["cwr"])
            if dc + 2 < 8:
                p.wait("sync", s["cmm"], (dc + 1) * 8)
                load_wc_dc(dc + 2, par)
            drain_wq(11, q="gpsimd")
        drain_wq(10 ** 9, q="gpsimd")

        p.wait("sync", s["wqw"])
        jj2 = 0
        for j in range(86):
            par = jj2 % 2
            if jj2 >= 2:
                p.wait("sync", s["agw"], (64 + jj2 - 1) * 16)
            p.dma("sync", wq_tr[par].ap()[:, :DS], wdq_sc[:, j * 128:(j + 1) * 128],
                  sem=s["wqt"], transpose=True)
            p.wait("scalar", s["wqt"])
            p.dma("scalar", agin_d[j * 128:(j + 1) * 128, :], wq_tr[par].ap()[:, :DS],
                  sem=s["agw"])
            jj2 += 1
        p.wait("gpsimd", s["agw"])
        p.op("gpsimd", lambda g: g.collective_compute(
            "AllGather", ALU.bypass, replica_groups=G8, ins=[agin_d], outs=[agout_d]),
            sem=s["ag2"])

        p.wait("sync", s["cmm"])
        p.wait("sync", s["agw"])
        for j in range(32):
            p.dma("sync", xqT_sb.ap()[:, j * TOK:(j + 1) * TOK], xq_sc[:, j * 128:(j + 1) * 128],
                  sem=s["xqt"], transpose=True)
        p.wait("tensor", s["xqt"])
        p.wait("tensor", s["ag0"], 1)
        p.wait("tensor", s["ag1"], 1)
        p.wait("sync", s["ag0"], 1)
        p.wait("sync", s["ag1"], 1)
        p.wait("gpsimd", s["cmm"])
        p.wait("scalar", s["cmm"])
        p.wait("scalar", s["agw"])
        p.wait("vector", s["wqw"])
        chunks = [(r, o, w) for r in range(8) for (o, w) in ((0, 512), (512, 512), (1024, 352))]
        ng = 0
        for (r, coff, cw) in chunks:
            if ng >= 1:
                p.wait("sync", s["gmm"], 8 * ng)
                p.wait("sync", s["umm"], 8 * ng)
            for kk in range(32):
                p.dma("sync", wgu_buf[0].ap()[:, kk * cw:(kk + 1) * cw],
                      agout_g[r * E + kk * 128:r * E + (kk + 1) * 128, coff:coff + cw],
                      sem=s["gw"])
                p.dma("sync", wgu_buf[1].ap()[:, kk * cw:(kk + 1) * cw],
                      agout_u[r * E + kk * 128:r * E + (kk + 1) * 128, coff:coff + cw],
                      sem=s["uw"])
            p.wait("tensor", s["gw"])
            p.wait("scalar", s["hmul"])
            for tt in range(8):
                slot = tt % 2
                ps = psA.ap()[:, slot * 512:slot * 512 + cw]
                if tt >= 2:
                    p.wait("tensor", s["sil"], s["sil"].n - 1)
                for kk in range(32):
                    p.op("tensor", lambda t, o=ps,
                         l=xqT_sb.ap()[:, kk * TOK + tt * 128:kk * TOK + (tt + 1) * 128],
                         r=wgu_buf[0].ap()[:, kk * cw:(kk + 1) * cw],
                         st=(kk == 0), sp=(kk == 31): t.matmul(o, l, r, start=st, stop=sp),
                         sem=(s["gmm"] if kk == 31 else None))
                p.wait("scalar", s["gmm"])
                p.op("scalar", lambda sc, o=gs_sb.ap()[:, tt * 512:tt * 512 + cw], i=ps:
                     sc.activation(o, i, AF.Silu), sem=s["sil"])
            p.wait("tensor", s["uw"])
            for tt in range(8):
                slot = 2 + tt % 2
                ps = psA.ap()[:, slot * 512:slot * 512 + cw]
                if tt >= 2:
                    p.wait("tensor", s["hmul"], s["hmul"].n - 1)
                for kk in range(32):
                    p.op("tensor", lambda t, o=ps,
                         l=xqT_sb.ap()[:, kk * TOK + tt * 128:kk * TOK + (tt + 1) * 128],
                         r=wgu_buf[1].ap()[:, kk * cw:(kk + 1) * cw],
                         st=(kk == 0), sp=(kk == 31): t.matmul(o, l, r, start=st, stop=sp),
                         sem=(s["umm"] if kk == 31 else None))
                p.wait("vector", s["umm"])
                p.wait("vector", s["sil"])
                p.wait("vector", s["hq"])
                p.op("vector", lambda v, i=ps, g=gs_sb.ap()[:, tt * 512:tt * 512 + cw],
                     o=q_in.ap()[:, :cw]: v.tensor_tensor(o, g, i, ALU.mult), sem=s["hmul"])
                p.wait("vector", s["hqw"])
                emit_quant(q_in.ap()[:, :cw], cw, q_out.ap()[:, :cw], sem_out=s["hq"])
                if r == 7 and coff == 1024:
                    p.op("vector", lambda v, ww=cw: v.tensor_copy(
                        q_out.ap()[:, ww - 128:ww], q_in.ap()[:, ww - 128:ww]), sem=s["hq"])
                p.wait("sync", s["hq"])
                p.dma("sync", hq_sc[tt * 128:(tt + 1) * 128, r * FS + coff:r * FS + coff + cw],
                      q_out.ap()[:, :cw], sem=s["hqw"])
            ng += 1

        p.wait("sync", s["hqw"])
        p.wait("sync", s["cwr"])
        p.wait("tensor", s["ag2"], 1)
        p.wait("sync", s["gmm"])
        p.wait("sync", s["umm"])
        p.wait("gpsimd", s["umm"])
        p.wait("gpsimd", s["ag2"], 1)
        WDG = [16, 16, 16, 16, 16, 6]
        nfm = 0
        gcnt = 0
        dgrp_after = []
        for tg in range(2):
            if tg == 1:
                p.wait("sync", s["dgrp"], dgrp_after[-1])
            for ft in range(86):
                p.dma("sync", hqT_sb.ap()[:, ft * 512:(ft + 1) * 512],
                      hq_sc[tg * 512:(tg + 1) * 512, ft * 128:(ft + 1) * 128],
                      sem=s["hqt"], transpose=True)
            p.wait("tensor", s["hqt"])
            for dc in range(8):
                gbase = 0
                for gi, gn in enumerate(WDG):
                    par = gcnt % 2
                    if gcnt >= 2:
                        p.wait("gpsimd", s["dgrp"], gcnt - 1)
                    p.dma("gpsimd", wd_buf[par].ap()[:, :gn * 512]
                          .rearrange("p (kk d) -> p kk d", kk=gn),
                          agout_d[dc * F + gbase * 128:dc * F + (gbase + gn) * 128, :]
                          .rearrange("(kk p) d -> p kk d", p=128), sem=s["dw"])
                    p.wait("tensor", s["dw"])
                    for tt in range(4):
                        slot = tt
                        ps = psB.ap()[:, slot * 512:(slot + 1) * 512]
                        if gi == 0 and nfm + tt >= 4:
                            p.wait("tensor", s["fin"], (nfm + tt) - 3)
                        for k2 in range(gn):
                            kk = gbase + k2
                            last = (tt == 3 and k2 == gn - 1)
                            p.op("tensor", lambda t, o=ps,
                                 l=hqT_sb.ap()[:, kk * 512 + tt * 128:kk * 512 + (tt + 1) * 128],
                                 r=wd_buf[par].ap()[:, k2 * 512:(k2 + 1) * 512],
                                 st=(kk == 0), sp=False: t.matmul(o, l, r, start=st, stop=sp),
                                 sem=(s["dgrp"] if last else None))
                    gbase += gn
                    gcnt += 1
                    dgrp_after.append(s["dgrp"].n)
                for tt in range(4):
                    ps = psB.ap()[:, tt * 512:(tt + 1) * 512]
                    row0 = tg * 512 + tt * 128
                    p.op("tensor", lambda t, o=ps,
                         l=t1T_sb.ap()[:, row0:row0 + 128],
                         r=lrB_bf.ap()[:, dc * 512:(dc + 1) * 512]:
                         t.matmul(o, l, r, start=False, stop=True), sem=s["dmm"])
                    fpar = nfm % 2
                    if nfm >= 2:
                        p.wait("sync", s["fin"], nfm - 1)
                    p.dma("sync", x_fin[fpar].ap(), x_own[row0:row0 + 128, dc * 512:(dc + 1) * 512],
                          sem=s["fld"])
                    p.dma("sync", rs_fin[fpar].ap(),
                          wcout_sc[row0:row0 + 128, dc * 512:(dc + 1) * 512], sem=s["fld"])
                    p.wait("vector", s["fld"])
                    p.wait("vector", s["dmm"])
                    p.op("vector", lambda v, i=ps, x=x_fin[fpar].ap(),
                         o=o_fin[fpar].ap(): v.scalar_tensor_tensor(
                        o, i, 1.0, x, ALU.bypass, ALU.add))
                    p.op("vector", lambda v, a=o_fin[fpar].ap(), b=rs_fin[fpar].ap(),
                         o=x_fin[fpar].ap(): v.tensor_tensor(o, a, b, ALU.add), sem=s["fin"])
                    p.wait("sync", s["fin"])
                    p.dma("sync", out_own[row0:row0 + 128, dc * 512:(dc + 1) * 512],
                          x_fin[fpar].ap(), sem=s["out"])
                    nfm += 1

        p.wait("gpsimd", s["out"])

        block = es.enter_context(nc.Block())

        @block.sync
        def _(eng):
            for fn in p.sched["sync"]:
                fn(eng)

        @block.scalar
        def _(eng):
            for fn in p.sched["scalar"]:
                fn(eng)

        @block.vector
        def _(eng):
            for fn in p.sched["vector"]:
                fn(eng)

        @block.tensor
        def _(eng):
            for fn in p.sched["tensor"]:
                fn(eng)

        @block.gpsimd
        def _(eng):
            for fn in p.sched["gpsimd"]:
                fn(eng)

    _CACHE["nc"] = nc
    return nc


def kernel(L0_attn_hidden_states, L0_attn_w_qkv, L0_attn_w_c, L0_ffn_w_gate,
           L0_ffn_w_up, L0_ffn_w_down, L0_ffn_mlp_output_scale, L0_ffn_lr_A,
           L0_ffn_lr_B, L0_ffn_up_reorder_index, L0_ffn_down_reorder_index):
    x = np.asarray(L0_attn_hidden_states, dtype=np.float32)
    w_qkv = np.asarray(L0_attn_w_qkv, dtype=np.float32)
    w_c = np.asarray(L0_attn_w_c, dtype=np.float32)
    wg = np.asarray(L0_ffn_w_gate, dtype=np.float32)
    wu = np.asarray(L0_ffn_w_up, dtype=np.float32)
    wd = np.asarray(L0_ffn_w_down, dtype=np.float32)
    mscale = np.asarray(L0_ffn_mlp_output_scale, dtype=np.float32)
    lr_A = np.asarray(L0_ffn_lr_A, dtype=np.float32)
    lr_B = np.asarray(L0_ffn_lr_B, dtype=np.float32)
    uperm = np.asarray(L0_ffn_up_reorder_index, dtype=np.int64)
    dperm = np.asarray(L0_ffn_down_reorder_index, dtype=np.int64)

    x_eff = x[:, :, uperm]
    wqkv_eff = w_qkv[uperm, :]
    wcT_eff = np.ascontiguousarray(w_c[:, uperm].T)
    wg_eff = wg[dperm][:, uperm]
    wu_eff = wu[dperm][:, uperm]
    wd_eff = wd[uperm][:, dperm]
    scale_eff = mscale[uperm]
    lrA_eff = np.ascontiguousarray(lr_A[uperm, :])
    lrB_eff = np.ascontiguousarray(lr_B[:, uperm])

    slopes = _alibi_slopes(32)
    nc = build_program()

    in_maps = []
    for c in range(8):
        b, g = c // 4, c % 4
        sl = slopes[4 * c:4 * c + 4]
        rr = np.arange(128, dtype=np.float32)
        cols = np.zeros((128, NHL * 8 * 32), np.float32)
        for h in range(NHL):
            for cc in range(8):
                for t in range(32):
                    cols[:, (h * 8 + cc) * 32 + t] = sl[h] * (128.0 * t + rr - 512.0 * cc)
        bq = np.zeros((2, NHL * 512), BF)
        qq = np.arange(512, dtype=np.float64)
        for h in range(NHL):
            bfull = (-float(sl[h]) * qq / SCAL).astype(np.float32)
            hi = bfull.astype(BF)
            lo = (bfull - hi.astype(np.float32)).astype(BF)
            bq[0, h * 512:(h + 1) * 512] = hi
            bq[1, h * 512:(h + 1) * 512] = lo
        uu = np.arange(896)
        msk = np.where(uu[None, :] >= (384 + np.arange(128)[:, None]),
                       np.float32(3e38), np.float32(-3e38)).astype(np.float32)
        in_maps.append({
            "x_own": np.ascontiguousarray(x_eff[b, 1024 * g:1024 * (g + 1)]),
            "wqkv": np.ascontiguousarray(
                np.hstack([wqkv_eff[:, 512 * c:512 * (c + 1)], wqkv_eff[:, 4096:4352]])),
            "wc": wcT_eff,
            "wg_sh": np.ascontiguousarray(wg_eff[FS * c:FS * (c + 1)]),
            "wu_sh": np.ascontiguousarray(wu_eff[FS * c:FS * (c + 1)]),
            "wd_sh": np.ascontiguousarray(wd_eff[DS * c:DS * (c + 1)]),
            "scale_sh": np.ascontiguousarray(scale_eff[DS * c:DS * (c + 1)].reshape(4, 128).T),
            "lrA": lrA_eff,
            "lrB": lrB_eff,
            "aux_bias": cols,
            "aux_bq": bq,
            "aux_mask": msk,
            "aux_ones": np.ones((128, 128), BF),
        })

    res = run_bass_kernel_spmd(nc, in_maps, core_ids=list(range(8)))

    out_eff = np.empty((2, S, E), np.float32)
    for c in range(8):
        b, g = c // 4, c % 4
        out_eff[b, 1024 * g:1024 * (g + 1)] = res.results[c]["out_own"]
    out = np.empty_like(out_eff)
    out[:, :, uperm] = out_eff
    return out
